# version 23
# baseline (speedup 1.0000x reference)
import numpy as np
import concourse.bass as bass
import concourse.mybir as mybir
from concourse.bass_utils import run_bass_kernel_spmd

F32 = mybir.dt.float32
F32R = mybir.dt.float32r
BF16 = mybir.dt.bfloat16
AF = mybir.ActivationFunctionType
OP = mybir.AluOpType

D = 1024
SEQ = 2048
NCORE = 8
CH = 256
NCH = SEQ // CH
EPS_MAC = 1e-5
EPS_ATT = 1e-6
INV_D = 1.0 / 1024.0

_COL_NAMES = ["ln1w", "ln1b", "lngw", "lngb", "bg", "lncw", "lncb", "dec"]
_COL_OFF = {n: 8 * i for i, n in enumerate(_COL_NAMES)}
_COL_W = 8 * len(_COL_NAMES)


def _sigma():
    P = np.zeros(D, dtype=np.int64)
    for d_ in range(32):
        for g_ in range(16):
            P[32 * g_ + d_] = 16 * d_ + g_
            P[512 + 32 * g_ + d_] = 512 + 16 * d_ + g_
    return P


def _pi_consts():
    pis = np.zeros((8, 128, 128), dtype=np.float32)
    for neg in range(2):
        for par in range(2):
            for km in range(2):
                j = neg * 4 + par * 2 + km
                for p in range(128):
                    if (p // 32) % 2 != par:
                        continue
                    k = 64 * km + 32 * (p // 64) + (p % 32)
                    pis[j, k, p] = -1.0 if neg else 1.0
    return pis


def _col_pack(vec):
    return np.ascontiguousarray(vec.reshape(8, 128).T.astype(np.float32))


class _Step:
    __slots__ = ("engine", "fn", "deps", "idx", "is_dma", "sem_name", "dma_cum", "order")


class Prog:
    def __init__(self):
        self.steps = []
        self.writer = {}
        self.readers = {}
        self.dma_counts = {}

    def step(self, engine, fn, reads=(), writes=(), extra=(), dma_sem=None, ndma=1):
        st = _Step()
        st.engine = engine
        st.fn = fn
        st.deps = set()
        st.order = len(self.steps)
        for b in reads:
            w = self.writer.get(b)
            if w is not None:
                st.deps.add(w)
        for b in writes:
            w = self.writer.get(b)
            if w is not None:
                st.deps.add(w)
            for r in self.readers.get(b, ()):
                st.deps.add(r)
        for e in extra:
            if e is not None:
                st.deps.add(e)
        st.deps.discard(st)
        for b in reads:
            self.readers.setdefault(b, []).append(st)
        for b in writes:
            self.writer[b] = st
            self.readers[b] = [st]
        st.is_dma = dma_sem is not None
        st.sem_name = dma_sem
        if st.is_dma:
            self.dma_counts[dma_sem] = self.dma_counts.get(dma_sem, 0) + ndma
            st.dma_cum = self.dma_counts[dma_sem]
        for d in st.deps:
            assert d.order < st.order, "dependency on later step"
        self.steps.append(st)
        return st

    def finalize(self):
        idx = {}
        for st in self.steps:
            if not st.is_dma:
                idx[st.engine] = idx.get(st.engine, 0) + 1
                st.idx = idx[st.engine]

    def emit(self, engine, eng, sems, qsems):
        waited = {}
        for st in self.steps:
            if st.engine != engine:
                continue
            for d in sorted(st.deps, key=lambda s: s.order):
                if d.is_dma:
                    key = ("q", d.sem_name)
                    val = 16 * d.dma_cum
                    sem = qsems[d.sem_name]
                else:
                    if d.engine == engine:
                        continue
                    key = ("e", d.engine)
                    val = d.idx
                    sem = sems[d.engine]
                if waited.get(key, 0) < val:
                    eng.wait_ge(sem, val)
                    waited[key] = val
            ret = st.fn(eng)
            if not st.is_dma:
                if engine == "tensor" and ret is not None:
                    ret.then_inc(sems[engine], 1)
                elif engine == "tensor":
                    eng.nop().then_inc(sems[engine], 1)
                else:
                    eng.drain()
                    eng.sem_inc(sems[engine], 1)


def build_program():
    nc = bass.Bass("TRN2", target_bir_lowering=False)

    x_hbm = nc.dram_tensor("x", [D, SEQ], F32, kind="ExternalInput").ap()
    y_hbm = nc.dram_tensor("y", [D, SEQ], F32, kind="ExternalOutput").ap()
    wt_hbm = nc.dram_tensor("wt", [D, D], F32, kind="ExternalInput").ap()
    wo_hbm = nc.dram_tensor("wo", [D, D], F32, kind="ExternalInput").ap()
    wg_hbm = nc.dram_tensor("wg", [2 * D, D], F32, kind="ExternalInput").ap()
    cols_hbm = nc.dram_tensor("cols", [128, _COL_W], F32, kind="ExternalInput").ap()
    rowsc_hbm = nc.dram_tensor("rowsc", [3, D], F32, kind="ExternalInput").ap()
    pis_hbm = nc.dram_tensor("pis", [8 * 128, 128], F32, kind="ExternalInput").ap()
    g2l_r_h = nc.dram_tensor("g2l_r", [4, 256], F32, kind="ExternalInput").ap()
    g2l_i_h = nc.dram_tensor("g2l_i", [4, 256], F32, kind="ExternalInput").ap()
    g1r_r_h = nc.dram_tensor("g1r_r", [4, 1024], F32, kind="ExternalInput").ap()
    g1r_i_h = nc.dram_tensor("g1r_i", [4, 1024], F32, kind="ExternalInput").ap()
    wgspill = nc.dram_tensor("wgspill", [16, 128, 2048], BF16, kind="Internal").ap()

    import contextlib
    es = contextlib.ExitStack()
    with es:
        def sb(name, shape, dt=F32):
            return es.enter_context(nc.sbuf_tensor(name, shape, dt))

        def psum(name, shape):
            return es.enter_context(nc.psum_tensor(name, shape, F32))

        b1 = sb("b1", [128, 2048])
        b2 = sb("b2", [128, 2048])
        b3 = sb("b3", [128, 2048])
        b5 = sb("b5", [128, 2048])
        b6 = sb("b6", [128, 2048])

        wffn_h = sb("wffn_h", [128, 8192], BF16)
        wffn_l = sb("wffn_l", [128, 8192], BF16)
        wt_h = sb("wt_h", [128, 8192], BF16)
        wt_l = sb("wt_l", [128, 8192], BF16)
        wo_h = sb("wo_h", [128, 8192], BF16)
        wo_l = sb("wo_l", [128, 8192], BF16)
        wgs = [sb(f"wgs{i}", [128, 8192], BF16) for i in range(2)]
        spA = sb("spA", [128, 4096], BF16)
        spBC = sb("spBC", [128, 4096], BF16)

        sq_sb = sb("sq_sb", [128, 2048], F32R)
        cols = sb("cols_sb", [128, _COL_W])
        carry = sb("carry", [128, 8])
        ones_bf = sb("ones_bf", [128, 128], BF16)
        ones_r = sb("ones_r", [128, 128], F32R)
        ones_row = sb("ones_row", [1, 128])
        ones_col = sb("ones_col", [128, 1])
        a_sb = sb("a_sb", [128, CH])
        a2_sb = sb("a2_sb", [128, CH])
        r2 = sb("r2", [66, CH])
        r_t = sb("r_t", [66, CH])
        r_a = sb("r_a", [66, CH])
        r_rho = sb("r_rho", [2, CH])
        r_lm = a_sb
        w2all = sb("w2all", [66, 2048])
        W2F, W2T, W2O = 0, 32, 64
        W2G = (0, 1024)
        g2l_r = wffn_h[0:4, 0:512].bitcast(F32)
        g2l_i = wffn_l[0:4, 0:512].bitcast(F32)
        g1r_r = wffn_h[0:4, 512:2560].bitcast(F32)
        g1r_i = wffn_l[0:4, 512:2560].bitcast(F32)

        s_sb = wgs[0][:, 0:4096].bitcast(F32)
        u_r = wgs[0][:, 4096:8192].bitcast(F32)
        u_i = wgs[1][:, 0:4096].bitcast(F32)
        pis_sb = wgs[1][:, 4096:6144].bitcast(F32)

        pmain = [psum(f"pm{i}", [128, 512]) for i in range(4)]
        pstat = psum("pstat", [128, 512])
        prb = psum("prb", [128, 512])
        pmb = psum("pmb", [128, 512])
        pdec = psum("pdec", [128, 512])

        sem_T = es.enter_context(nc.semaphore("s_T"))
        sem_V = es.enter_context(nc.semaphore("s_V"))
        sem_S = es.enter_context(nc.semaphore("s_S"))
        sem_G = es.enter_context(nc.semaphore("s_G"))
        q_c = es.enter_context(nc.semaphore("q_c"))
        q_wA = es.enter_context(nc.semaphore("q_wA"))
        q_wB = es.enter_context(nc.semaphore("q_wB"))
        q_x = es.enter_context(nc.semaphore("q_x"))
        q_y = es.enter_context(nc.semaphore("q_y"))
        q_s = [es.enter_context(nc.semaphore(f"q_s{i}")) for i in range(4)]
        q_r = es.enter_context(nc.semaphore("q_r"))
        block = es.enter_context(nc.Block())

        sems = {"tensor": sem_T, "vector": sem_V, "scalar": sem_S, "gpsimd": sem_G}
        qsems = {"q_c": q_c, "q_wA": q_wA, "q_wB": q_wB, "q_x": q_x, "q_y": q_y, "q_r": q_r,
                 "q_s0": q_s[0], "q_s1": q_s[1], "q_s2": q_s[2], "q_s3": q_s[3]}

        P = Prog()

        def sl(slab, kt, w=CH):
            return slab[:, w * kt:w * (kt + 1)]

        def ccol(name, kt):
            o = _COL_OFF[name] + kt
            return cols[:, o:o + 1]

        def bcast8(prow):
            return prow.unsqueeze(1).broadcast_to((128, 8, CH))

        def v8(slab):
            return slab.ap().rearrange("p (c s) -> p c s", s=CH)

        def f_consts(eng):
            eng.dma_start(out=cols[:], in_=cols_hbm).then_inc(q_c, 16)
            eng.dma_start(out=pis_sb.rearrange("p (j k) -> p j k", j=8),
                          in_=pis_hbm.rearrange("(j p) k -> p j k", p=128)).then_inc(q_c, 16)
            eng.dma_start(out=g2l_r[:], in_=g2l_r_h).then_inc(q_c, 16)
            eng.dma_start(out=g2l_i[:], in_=g2l_i_h).then_inc(q_c, 16)
            eng.dma_start(out=g1r_r[:], in_=g1r_r_h).then_inc(q_c, 16)
            eng.dma_start(out=g1r_i[:], in_=g1r_i_h).then_inc(q_c, 16)
            eng.dma_start(out=w2all[W2F + 1:W2F + 2, 0:1024], in_=rowsc_hbm[0:1, :]).then_inc(q_c, 16)
            eng.dma_start(out=b2[0:1, 1024:2048], in_=rowsc_hbm[1:2, :]).then_inc(q_c, 16)
        P.step("sync", f_consts, writes=["cols", "wgs1", "wffn_h", "wffn_l", "w2all", "b2"],
               dma_sem="q_c", ndma=8)

        def f_init(eng):
            eng.memset(ones_bf[:], 1.0)
            eng.memset(ones_row[:], 1.0)
            eng.memset(ones_col[:], 1.0)
            eng.memset(carry[:], 0.0)
            eng.memset(pdec[:], 1.0)
            eng.drain()
            eng.tensor_copy(ones_r[:].bitcast(F32R), ones_bf[:])
            eng.drain()
        P.step("vector", f_init, writes=["ones", "carry", "pdec"])

        def f_dec(eng):
            for g in range(8):
                eng.tensor_scalar(pdec[:, 64 * g:64 * (g + 1)],
                                  pdec[:, 64 * g:64 * (g + 1)],
                                  ccol("dec", g), None, OP.mult)
            eng.drain()
        P.step("vector", f_dec, reads=["cols", "pdec"], writes=["pdec"])

        for ri, (g2l_t, g1r_t, u_t, uname) in enumerate(
                [(g2l_r, g1r_r, u_r, "wgs0"), (g2l_i, g1r_i, u_i, "wgs1")]):
            for t in range(2):
                for n in range(2):
                    pb = pmain[(t * 2 + n) % 2]
                    pname = f"pm{(t * 2 + n) % 2}"

                    def f_smm(eng, g2l_t=g2l_t, g1r_t=g1r_t, t=t, n=n, pb=pb):
                        return eng.matmul(pb[:, 0:512], g2l_t[:, 128 * t:128 * (t + 1)],
                                          g1r_t[:, 512 * n:512 * (n + 1)],
                                          start=True, stop=True)
                    P.step("tensor", f_smm, reads=["wffn_h", "wffn_l"], writes=[pname])

                    def f_sev(eng, t=t, n=n, pb=pb):
                        eng.copy(s_sb[:, 1024 * t + 512 * n:1024 * t + 512 * n + 512],
                                 pb[:, 0:512])
                    P.step("scalar", f_sev, reads=[pname], writes=["wgs0"])

            def f_tr(eng, u_t=u_t):
                eng.transpose(u_t[:, 0:1024], s_sb[:, 0:1024])
                eng.transpose(u_t[:, 1024:2048], s_sb[:, 1024:2048])
                eng.drain()
            P.step("vector", f_tr, reads=["wgs0"], writes=[uname])

        for kt in range(8):
            ktl = kt % 4
            tu = ktl // 2
            km = ktl % 2
            for half in range(2):
                if kt < 4:
                    u_t, uname, neg = (u_r, "wgs0", 0) if half == 0 else (u_i, "wgs1", 0)
                else:
                    u_t, uname, neg = (u_i, "wgs1", 1) if half == 0 else (u_r, "wgs0", 0)
                bi = 2 + (2 * kt + half) % 2
                pb, pbn = pmain[bi], f"pm{bi}"

                def f_pmm(eng, u_t=u_t, tu=tu, km=km, neg=neg, pb=pb):
                    uview = u_t[:, 1024 * tu:1024 * (tu + 1)].rearrange(
                        "p (b two e) -> p two e b", b=32, two=2, e=16)
                    ins = None
                    for par in range(2):
                        j = neg * 4 + par * 2 + km
                        ins = eng.matmul(pb[:, 0:512], pis_sb[:, 128 * j:128 * (j + 1)],
                                         uview[:, par, :, :], start=(par == 0), stop=(par == 1))
                    return ins
                P.step("tensor", f_pmm, reads=[uname, "wgs1"], writes=[pbn])

                def f_pevh(eng, kt=kt, half=half, pb=pb):
                    o = 1024 * kt + 512 * half
                    eng.activation(wffn_h[:, o:o + 512], pb[:, 0:512], AF.Copy)
                P.step("scalar", f_pevh, reads=[pbn], writes=["wffn_h"])

                def f_pevl(eng, kt=kt, half=half, pb=pb):
                    o = 1024 * kt + 512 * half
                    eng.tensor_tensor(wffn_l[:, o:o + 512], pb[:, 0:512],
                                      wffn_h[:, o:o + 512], OP.subtract)
                    eng.drain()
                P.step("vector", f_pevl, reads=[pbn, "wffn_h"], writes=["wffn_l"])

        def f_w1f(eng):
            ins = None
            for half in range(2):
                pb = pmain[half]
                for kt in range(8):
                    eng.matmul(pb[0:1, 0:512], ones_bf[:, 0:1],
                               wffn_h[:, 1024 * kt + 512 * half:1024 * kt + 512 * (half + 1)],
                               start=(kt == 0), stop=False)
                    ins = eng.matmul(pb[0:1, 0:512], ones_bf[:, 0:1],
                                     wffn_l[:, 1024 * kt + 512 * half:1024 * kt + 512 * (half + 1)],
                                     start=False, stop=(kt == 7))
            return ins
        P.step("tensor", f_w1f, reads=["wffn_h", "wffn_l", "ones"],
               writes=["pm0", "pm1"])

        def f_w1fev(eng):
            eng.activation(w2all[W2F:W2F + 1, 0:512], pmain[0][0:1, 0:512], AF.Copy)
            eng.activation(w2all[W2F:W2F + 1, 512:1024], pmain[1][0:1, 0:512], AF.Copy)
        P.step("scalar", f_w1fev, reads=["pm0", "pm1"], writes=["w2all"])

        wt_v = wt_hbm.rearrange("(kt p) m -> kt p m", p=128)
        wo_v = wo_hbm.rearrange("(kt p) m -> kt p m", p=128)
        wg_v = wg_hbm.rearrange("(kt p) m -> kt p m", p=128)

        stage_bufs = [(b3, "b3"), (b5, "b5"), (b6, "b6"), (b1, "b1")]

        def f_bbc(eng):
            eng.matmul(pmain[2][:, 0:512], ones_row[0:1, :], b2[0:1, 1024:1536],
                       start=True, stop=True)
            return eng.matmul(pmain[3][:, 0:512], ones_row[0:1, :], b2[0:1, 1536:2048],
                              start=True, stop=True)
        P.step("tensor", f_bbc, reads=["b2", "ones"], writes=["pm2", "pm3"])

        def emit_wsplit(tag, src_v, kts, dsth, dsthname, dstl, dstlname,
                        scale_col, bias_col, beta_scale, rot0=0):
            for i0 in range(0, len(kts), 2):
                pair = kts[i0:i0 + 2]
                ri = (rot0 + i0 // 2) % 4
                stg, sn = stage_bufs[ri]
                qs = f"q_s{ri}"

                def f_ld(eng, pair=pair, stg=stg, qs=qs):
                    for j, kt in enumerate(pair):
                        eng.dma_start(out=stg[:, 1024 * j:1024 * (j + 1)],
                                      in_=src_v[kt]).then_inc(qsems[qs], 16)
                P.step("sync", f_ld, writes=[sn], dma_sem=qs, ndma=len(pair))

                if beta_scale:
                    def f_bs(eng, pair=pair, stg=stg):
                        for j in range(len(pair)):
                            eng.tensor_tensor(stg[:, 1024 * j:1024 * j + 512],
                                              stg[:, 1024 * j:1024 * j + 512],
                                              pmain[2][:, 0:512], OP.mult)
                            eng.tensor_tensor(stg[:, 1024 * j + 512:1024 * (j + 1)],
                                              stg[:, 1024 * j + 512:1024 * (j + 1)],
                                              pmain[3][:, 0:512], OP.mult)
                        eng.drain()
                    P.step("vector", f_bs, reads=[sn, "pm2", "pm3"], writes=[sn])

                def f_brow(eng, pair=pair, stg=stg):
                    ins = None
                    for j, kt in enumerate(pair):
                        eng.matmul(pstat[0:1, 0:512], ccol(bias_col, kt),
                                   stg[:, 1024 * j:1024 * j + 512],
                                   start=(kt == 0), stop=(kt == 7))
                        ins = eng.matmul(prb[0:1, 0:512], ccol(bias_col, kt),
                                         stg[:, 1024 * j + 512:1024 * (j + 1)],
                                         start=(kt == 0), stop=(kt == 7))
                    return ins
                P.step("tensor", f_brow, reads=[sn, "cols"], writes=["pstat", "prb"])

                def f_sc(eng, pair=pair, stg=stg):
                    for j, kt in enumerate(pair):
                        eng.tensor_scalar(stg[:, 1024 * j:1024 * (j + 1)],
                                          stg[:, 1024 * j:1024 * (j + 1)],
                                          ccol(scale_col, kt), None, OP.mult)
                    eng.drain()
                P.step("vector", f_sc, reads=[sn, "cols"], writes=[sn])

                def f_w1(eng, pair=pair, stg=stg):
                    ins = None
                    for j, kt in enumerate(pair):
                        eng.matmul(pmain[0][0:1, 0:512], ones_col[:],
                                   stg[:, 1024 * j:1024 * j + 512],
                                   start=(kt == 0), stop=(kt == 7))
                        ins = eng.matmul(pmain[1][0:1, 0:512], ones_col[:],
                                         stg[:, 1024 * j + 512:1024 * (j + 1)],
                                         start=(kt == 0), stop=(kt == 7))
                    return ins
                P.step("tensor", f_w1, reads=[sn, "ones"], writes=["pm0", "pm1"])

                def f_hi(eng, pair=pair, stg=stg):
                    for j, kt in enumerate(pair):
                        eng.activation(sl(dsth, kt, 1024),
                                       stg[:, 1024 * j:1024 * (j + 1)], AF.Copy)
                P.step("scalar", f_hi, reads=[sn], writes=[dsthname])

                def f_lo(eng, pair=pair, stg=stg):
                    for j, kt in enumerate(pair):
                        eng.tensor_tensor(sl(dstl, kt, 1024),
                                          stg[:, 1024 * j:1024 * (j + 1)],
                                          sl(dsth, kt, 1024), OP.subtract)
                    eng.drain()
                P.step("vector", f_lo, reads=[sn, dsthname], writes=[dstlname])

        def emit_wrows(base):
            def f_ev(eng):
                eng.activation(w2all[base:base + 1, 0:512], pmain[0][0:1, 0:512], AF.Copy)
                eng.activation(w2all[base:base + 1, 512:1024], pmain[1][0:1, 0:512], AF.Copy)
                eng.activation(b2[0:1, 0:512], pstat[0:1, 0:512], AF.Copy)
                eng.activation(b2[0:1, 512:1024], prb[0:1, 0:512], AF.Copy)
            P.step("scalar", f_ev, reads=["pm0", "pm1", "pstat", "prb"],
                   writes=["w2all", "b2"])

            def f_dma(eng):
                eng.dma_start(out=w2all[base + 1:base + 2, 0:1024],
                              in_=b2[0:1, 0:1024]).then_inc(q_c, 16)
            P.step("sync", f_dma, reads=["b2"], writes=["w2all"], dma_sem="q_c")

        emit_wsplit("wt", wt_v, list(range(8)), wt_h, "wt_h", wt_l, "wt_l",
                    scale_col="ln1w", bias_col="ln1b", beta_scale=True, rot0=0)
        emit_wrows(W2T)
        emit_wsplit("wo", wo_v, list(range(8)), wo_h, "wo_h", wo_l, "wo_l",
                    scale_col="lncw", bias_col="lncb", beta_scale=False, rot0=0)
        emit_wrows(W2O)

        def f_bgdma(eng):
            eng.dma_start(out=b2[0:1, 1024:2048],
                          in_=rowsc_hbm[2:3, :]).then_inc(q_c, 16)
        P.step("sync", f_bgdma, writes=["b2"], dma_sem="q_c")

        wg_stage = [(b3, "b3"), (b5, "b5")]
        wg_out = [(spBC, "spBC"), (spA, "spA")]

        def emit_wg_pair(i0):
            pair = [i0, i0 + 1]
            ri = (i0 // 2) % 2
            stg, sn = wg_stage[ri]
            out, on = wg_out[ri]
            qs = f"q_s{ri}"

            def f_ld(eng, pair=pair, stg=stg, qs=qs):
                for j, kt in enumerate(pair):
                    eng.dma_start(out=stg[:, 1024 * j:1024 * (j + 1)],
                                  in_=wg_v[kt]).then_inc(qsems[qs], 16)
            P.step("sync", f_ld, writes=[sn], dma_sem=qs, ndma=2)

            if i0 < 8:
                def f_gbrow(eng, pair=pair, stg=stg):
                    ins = None
                    for j, kt in enumerate(pair):
                        eng.matmul(pstat[0:1, 0:512], ccol("ln1b", kt),
                                   stg[:, 1024 * j:1024 * j + 512],
                                   start=(kt == 0), stop=(kt == 7))
                        ins = eng.matmul(prb[0:1, 0:512], ccol("ln1b", kt),
                                         stg[:, 1024 * j + 512:1024 * (j + 1)],
                                         start=(kt == 0), stop=(kt == 7))
                    return ins
                P.step("tensor", f_gbrow, reads=[sn, "cols"], writes=["pstat", "prb"])

                def f_gsc(eng, pair=pair, stg=stg):
                    for j, kt in enumerate(pair):
                        eng.tensor_scalar(stg[:, 1024 * j:1024 * (j + 1)],
                                          stg[:, 1024 * j:1024 * (j + 1)],
                                          ccol("ln1w", kt), None, OP.mult)
                    eng.drain()
                P.step("vector", f_gsc, reads=[sn, "cols"], writes=[sn])

                def f_gw1(eng, pair=pair, stg=stg):
                    ins = None
                    for j, kt in enumerate(pair):
                        eng.matmul(pmain[0][0:1, 0:512], ones_col[:],
                                   stg[:, 1024 * j:1024 * j + 512],
                                   start=(kt == 0), stop=(kt == 7))
                        ins = eng.matmul(pmain[1][0:1, 0:512], ones_col[:],
                                         stg[:, 1024 * j + 512:1024 * (j + 1)],
                                         start=(kt == 0), stop=(kt == 7))
                    return ins
                P.step("tensor", f_gw1, reads=[sn, "ones"], writes=["pm0", "pm1"])

            def f_hi(eng, pair=pair, stg=stg, out=out):
                for j, kt in enumerate(pair):
                    eng.activation(out[:, 2048 * j:2048 * j + 1024],
                                   stg[:, 1024 * j:1024 * (j + 1)], AF.Copy)
            P.step("scalar", f_hi, reads=[sn], writes=[on])

            def f_lo(eng, pair=pair, stg=stg, out=out):
                for j, kt in enumerate(pair):
                    eng.tensor_tensor(out[:, 2048 * j + 1024:2048 * (j + 1)],
                                      stg[:, 1024 * j:1024 * (j + 1)],
                                      out[:, 2048 * j:2048 * j + 1024], OP.subtract)
                eng.drain()
            P.step("vector", f_lo, reads=[sn, on], writes=[on])

            def f_sp(eng, pair=pair, out=out):
                for j, kt in enumerate(pair):
                    eng.dma_start(out=wgspill[kt],
                                  in_=out[:, 2048 * j:2048 * (j + 1)]).then_inc(q_wB, 16)
            P.step("sync", f_sp, reads=[on], writes=["wgspill"],
                   dma_sem="q_wB", ndma=2)

        for i0 in range(0, 8, 2):
            emit_wg_pair(i0)

        def f_gwev(eng):
            eng.activation(w2all[0:1, 1024:1536], pmain[0][0:1, 0:512], AF.Copy)
            eng.activation(w2all[0:1, 1536:2048], pmain[1][0:1, 0:512], AF.Copy)
            eng.activation(b2[0:1, 0:512], pstat[0:1, 0:512], AF.Copy)
            eng.activation(b2[0:1, 512:1024], prb[0:1, 0:512], AF.Copy)
        P.step("scalar", f_gwev, reads=["pm0", "pm1", "pstat", "prb"],
               writes=["w2all", "b2"])

        def f_bgadd(eng):
            eng.tensor_tensor(b2[0:1, 0:1024], b2[0:1, 0:1024],
                              b2[0:1, 1024:2048], OP.add)
            eng.drain()
        P.step("vector", f_bgadd, reads=["b2"], writes=["b2"])

        def f_gwdma(eng):
            eng.dma_start(out=w2all[1:2, 1024:2048],
                          in_=b2[0:1, 0:1024]).then_inc(q_c, 16)
        P.step("sync", f_gwdma, reads=["b2"], writes=["w2all"], dma_sem="q_c")

        def sp_hi(sp, kt):
            return sp[:, CH * kt:CH * (kt + 1)]

        def sp_lo(sp, kt):
            return sp[:, 2048 + CH * kt:2048 + CH * (kt + 1)]

        def wtile(buf, kt):
            return buf[:, 1024 * kt:1024 * (kt + 1)]

        def wg_w(kt):
            buf = wgs[(kt // 4) % 2]
            lk = kt % 4
            return (f"wgs{(kt // 4) % 2}",
                    buf[:, 2048 * lk:2048 * lk + 1024],
                    buf[:, 2048 * lk + 1024:2048 * (lk + 1)])

        def emit_split(src, srcname, sp, spname, with_sq=True):
            def f_hi(eng):
                eng.activation(sp[:, 0:2048], src[:], AF.Copy)
            P.step("scalar", f_hi, reads=[srcname], writes=[spname])

            def f_lo(eng):
                eng.tensor_tensor(sp[:, 2048:4096], src[:], sp[:, 0:2048], OP.subtract)
                eng.drain()
            P.step("vector", f_lo, reads=[srcname, spname], writes=[spname])

            if with_sq:
                def f_sq(eng):
                    eng.activation(sq_sb[:], src[:], AF.Square)
                P.step("scalar", f_sq, reads=[srcname], writes=["sq"])

        def emit_stats(spname, sp):
            def f_st(eng):
                ins = None
                for half in range(2):
                    for kt in range(8):
                        eng.matmul(pstat[:, 0:CH], ones_bf[:],
                                   sp[:, 2048 * half + CH * kt:2048 * half + CH * (kt + 1)],
                                   start=(half == 0 and kt == 0), stop=(half == 1 and kt == 7))
                for kt in range(8):
                    ins = eng.matmul(pstat[:, CH:2 * CH], ones_r[:],
                                     sq_sb[:, CH * kt:CH * (kt + 1)],
                                     start=(kt == 0), stop=(kt == 7))
                return ins
            P.step("tensor", f_st, reads=[spname, "sq", "ones"], writes=["pstat"])

        def emit_rows(kind, half, need_corr=True):
            nl = 66 if need_corr else 1
            mu = r2 if need_corr else r_lm

            def f_row1(eng):
                eng.tensor_scalar(mu[0:nl, :], pstat[0:nl, 0:CH], -INV_D, None, OP.mult)
                eng.drain()
                eng.tensor_tensor(r_t[0:nl, :], mu[0:nl, :], mu[0:nl, :], OP.mult)
                eng.drain()
                eng.scalar_tensor_tensor(r_a[0:nl, :], pstat[0:nl, CH:2 * CH], INV_D,
                                         r_t[0:nl, :], OP.mult, OP.subtract)
                eng.drain()
                if kind == "fused":
                    eng.tensor_scalar(r_a[0:nl, :], r_a[0:nl, :], 1.0 + EPS_ATT,
                                      EPS_MAC * EPS_ATT, OP.mult, OP.add)
                    eng.drain()
                else:
                    eps = EPS_MAC if kind == "mac" else EPS_ATT
                    eng.tensor_scalar(r_a[0:nl, :], r_a[0:nl, :], eps, None, OP.add)
                    eng.drain()
            P.step("vector", f_row1, reads=["pstat"],
                   writes=(["r2", "rows", "rows2"] if need_corr
                           else ["a_sb", "rows", "rows2"]))

            def f_sqrt(eng):
                eng.activation(r_t[0:nl, :], r_a[0:nl, :], AF.Sqrt)
                if not need_corr:
                    eng.drain()
                    eng.activation(r_a[0:1, 0:1], r_a[0:1, 0:1], AF.Sigmoid)
            P.step("scalar", f_sqrt, reads=["rows"],
                   writes=(["rows2"] if need_corr else ["rows2", "rows"]))

            def f_row2(eng):
                eng.reciprocal(r_rho[0:1, :], r_t[0:1, :])
                eng.drain()
                if half:
                    eng.tensor_scalar(r_rho[0:1, :], r_rho[0:1, :], 0.5, None, OP.mult)
                    eng.drain()
            P.step("vector", f_row2, reads=["rows2"], writes=["rows3"])

            if need_corr:
                def f_rdma(eng):
                    for i in (1, 33, 65):
                        eng.dma_start(out=r2[i:i + 1, :],
                                      in_=r_t[0:1, :]).then_inc(q_r, 16)
                P.step("gpsimd", f_rdma, reads=["rows2", "r2"], writes=["r2"],
                       dma_sem="q_r", ndma=3)

        def emit_bc(negmu_bc, ab=None, abname=None, mu_lng=False):
            if ab is None:
                ab, abname = a_sb, "a_sb"
            mu, mun = (r_lm, "a_sb") if mu_lng else (r2, "r2")

            def f_bc(eng):
                ins = eng.matmul(prb[:, 0:CH], ones_row[0:1, :], r_rho[0:1, :],
                                 start=True, stop=True)
                if negmu_bc:
                    ins = eng.matmul(pmb[:, 0:CH], ones_row[0:1, :], mu[0:1, :],
                                     start=True, stop=True)
                return ins
            P.step("tensor", f_bc, reads=["rows3", mun, "ones"],
                   writes=["prb"] + (["pmb"] if negmu_bc else []))

            def f_acp(eng):
                eng.activation(ab[:], prb[:, 0:CH], AF.Copy)
            P.step("scalar", f_acp, reads=["prb"], writes=[abname])

        def mk_main(rhs, wts, mts, reg, close=False):
            rnames = sorted({n for n, _, _ in rhs})
            wnames = sorted({n for n, _, _ in wts})

            def f_mm(eng):
                ins = None
                for pas in range(3):
                    for i in range(8):
                        _, xh, xl = rhs[i]
                        _, wh, wl = wts[i]
                        xap = xh if pas in (0, 1) else xl
                        wsel = wh if pas in (0, 2) else wl
                        for mt in mts:
                            ins = eng.matmul(
                                pmain[mt % 4][:, reg:reg + CH],
                                wsel[:, 128 * mt:128 * (mt + 1)], xap,
                                start=(pas == 0 and i == 0),
                                stop=(close and pas == 2 and i == 7))
                return ins
            P.step("tensor", f_mm, reads=rnames + wnames,
                   writes=[f"pm{m % 4}" for m in mts])

        def mk_corr(base, mts, reg, coff=0):
            def f_c(eng):
                ins = None
                for mt in mts:
                    ins = eng.matmul(pmain[mt % 4][:, reg:reg + CH],
                                     w2all[base:base + 2,
                                           coff + 128 * mt:coff + 128 * (mt + 1)],
                                     r2[base:base + 2, 0:CH], start=False, stop=True)
                return ins
            P.step("tensor", f_c, reads=["r2", "w2all"],
                   writes=[f"pm{m % 4}" for m in mts])

        def mk_evac(dst, dstname, mts, reg, ab=None, abname=None):
            if ab is None:
                ab, abname = a_sb, "a_sb"

            def f_e(eng):
                for mt in mts:
                    eng.tensor_tensor(sl(dst, mt), pmain[mt % 4][:, reg:reg + CH],
                                      ab[:], OP.mult)
                eng.drain()
            P.step("vector", f_e,
                   reads=[f"pm{m % 4}" for m in mts] + [abname], writes=[dstname])

        def emit_site_mms(sp, spname, wts, base, dst, dstname,
                          kind, half, negmu_bc=False, ab=None, abname=None):
            rhs = [(spname, sp_hi(sp, k), sp_lo(sp, k)) for k in range(8)]
            emit_stats(spname, sp)
            emit_rows(kind, half)
            mk_main(rhs, wts, [0, 1, 2, 3], 0)
            emit_bc(negmu_bc, ab, abname)
            mk_corr(base, [0, 1, 2, 3], 0)
            mk_main(rhs, wts, [4, 5, 6, 7], CH)
            mk_corr(base, [4, 5, 6, 7], CH)
            mk_evac(dst, dstname, [0, 1, 2, 3], 0, ab, abname)
            mk_evac(dst, dstname, [4, 5, 6, 7], CH, ab, abname)

        xv = x_hbm.rearrange("(kt p) (c s) -> c p kt s", p=128, s=CH)
        yv = y_hbm.rearrange("(kt p) (c s) -> c p kt s", p=128, s=CH)

        wffn_wts = [("wffn_h", wtile(wffn_h, k), wtile(wffn_l, k)) for k in range(8)]
        wt_wts = [("wt_h", wtile(wt_h, k), wtile(wt_l, k)) for k in range(8)]
        wo_wts = [("wo_h", wtile(wo_h, k), wtile(wo_l, k)) for k in range(8)]

        def emit_wgl(q):
            def f_wgl(eng, q=q):
                eng.dma_start(
                    out=wgs[q % 2].ap().rearrange("p (kt m) -> p kt m", m=2048),
                    in_=wgspill[4 * q:4 * (q + 1)].rearrange("kt p m -> p kt m")
                ).then_inc(q_wB if q % 2 else q_wA, 16)
            P.step("sync", f_wgl, reads=["wgspill"], writes=[f"wgs{q % 2}"],
                   dma_sem=("q_wB" if q % 2 else "q_wA"))

        last_ystore = None
        for c in range(NCH):
            resA, rn = (b1, "b1") if c % 2 == 0 else (b6, "b6")
            zG, zn = (b6, "b6") if c % 2 == 0 else (b1, "b1")

            emit_wgl(0)
            emit_wgl(1)

            if c == 0:
                def f_xl0(eng):
                    eng.dma_start(out=b3.ap().rearrange("p (kt s) -> p kt s", s=CH),
                                  in_=xv[0]).then_inc(q_x, 16)
                P.step("gpsimd", f_xl0, writes=["b3"], dma_sem="q_x")
                emit_split(b3, "b3", spA, "spA")
                emit_site_mms(spA, "spA", wffn_wts, W2F, b5, "b5", "mac", True,
                              ab=a2_sb, abname="a2_sb")

            if c == 0:
                def f_ad1(eng, resA=resA):
                    eng.tensor_tensor(resA[:], b3[:], b5[:], OP.add)
                    eng.drain()
                P.step("vector", f_ad1, reads=["b3", "b5"], writes=[rn])
                emit_wg_pair(8)
                emit_wg_pair(10)
                emit_split(resA, rn, spA, "spA")

            emit_site_mms(spA, "spA", wt_wts, W2T, b2, "b2",
                          "fused", False, negmu_bc=True, ab=a2_sb, abname="a2_sb")
            if c == 0:
                emit_wg_pair(12)

            def f_scan(eng):
                for sub in range(4):
                    for g in range(8):
                        o = CH * g + 64 * sub
                        init = carry[:, g:g + 1] if sub == 0 else b2[:, o - 1:o]
                        eng.tensor_tensor_scan(b2[:, o:o + 64],
                                               pdec[:, 64 * g:64 * (g + 1)],
                                               b2[:, o:o + 64],
                                               init, OP.mult, OP.add)
                    eng.drain()
                for g in range(8):
                    eng.tensor_copy(carry[:, g:g + 1], b2[:, CH * g + CH - 1:CH * g + CH])
                eng.drain()
            P.step("vector", f_scan, reads=["b2", "pdec", "carry"],
                   writes=["b2", "carry"])

            wga_rhs = [("spA", sp_hi(spA, k), sp_lo(spA, k)) for k in range(8)]
            wga_wts = [wg_w(k) for k in range(8)]
            mk_main(wga_rhs, wga_wts, [0, 1, 2, 3], 0)
            mk_corr(W2G[0], [0, 1, 2, 3], 0, coff=W2G[1])
            mk_main(wga_rhs, wga_wts, [4, 5, 6, 7], CH)
            mk_corr(W2G[0], [4, 5, 6, 7], CH, coff=W2G[1])
            emit_split(b2, "b2", spBC, "spBC")
            mk_evac(zG, zn, [0, 1, 2, 3], 0, a2_sb, "a2_sb")
            mk_evac(zG, zn, [4, 5, 6, 7], CH, a2_sb, "a2_sb")
            if c == 0:
                emit_wg_pair(14)

            emit_wgl(2)
            emit_wgl(3)

            def f_xm(eng, resA=resA):
                eng.tensor_tensor(v8(b3), v8(resA), bcast8(pmb[:, 0:CH]), OP.add)
                eng.drain()
                eng.tensor_tensor(v8(b3), v8(b3),
                                  a2_sb[:].unsqueeze(1).broadcast_to((128, 8, CH)),
                                  OP.mult)
                eng.drain()
            P.step("vector", f_xm, reads=[rn, "a2_sb", "pmb"], writes=["b3"])

            def f_xaf(eng):
                for kt in range(8):
                    eng.activation(sl(b3, kt), sl(b3, kt), AF.Identity,
                                   bias=ccol("ln1b", kt), scale=ccol("ln1w", kt))
            P.step("scalar", f_xaf, reads=["b3", "cols"], writes=["b3"])

            emit_site_mms(spBC, "spBC", wo_wts, W2O, b5, "b5", "lnc", False)

            emit_split(b5, "b5", spA, "spA", with_sq=False)

            def f_pregate(eng, resA=resA):
                eng.tensor_tensor(b5[:], b5[:], b3[:], OP.subtract)
                eng.drain()
                eng.tensor_tensor(resA[:], resA[:], b3[:], OP.add)
                eng.drain()
            P.step("vector", f_pregate, reads=["b5", "b3", rn], writes=["b5", rn])

            nx1 = c < NCH - 1
            if nx1:
                def f_xl(eng, c=c):
                    eng.dma_start(out=b3.ap().rearrange("p (kt s) -> p kt s", s=CH),
                                  in_=xv[c + 1]).then_inc(q_x, 16)
                P.step("gpsimd", f_xl, writes=["b3"], dma_sem="q_x")
                emit_split(b3, "b3", spBC, "spBC")

            wgb_rhs = [("spA", sp_hi(spA, k), sp_lo(spA, k)) for k in range(8)]
            wgb_wts = [wg_w(8 + k) for k in range(8)]
            mk_main(wgb_rhs, wgb_wts, [0, 1, 2, 3], 0, close=True)
            mk_main(wgb_rhs, wgb_wts, [4, 5, 6, 7], CH, close=True)

            def f_wgbev(eng, zG=zG):
                for mt in range(8):
                    reg = 0 if mt < 4 else CH
                    eng.tensor_tensor(sl(zG, mt), sl(zG, mt),
                                      pmain[mt % 4][:, reg:reg + CH], OP.add)
                eng.drain()
            P.step("vector", f_wgbev,
                   reads=["pm0", "pm1", "pm2", "pm3", zn], writes=[zn])

            if nx1:
                x_rhs = [("spBC", sp_hi(spBC, k), sp_lo(spBC, k)) for k in range(8)]
                emit_stats("spBC", spBC)
                mk_main(x_rhs, wffn_wts, [0, 1, 2, 3], 0)

            b2h = b2.ap().bitcast(BF16)

            def f_zsh(eng, zG=zG):
                eng.activation(b2h[:, 0:2048], zG[:], AF.Copy)
            P.step("scalar", f_zsh, reads=[zn], writes=["b2"])

            def f_zsl(eng, zG=zG):
                eng.tensor_tensor(b2h[:, 2048:4096], zG[:], b2h[:, 0:2048], OP.subtract)
                eng.drain()
            P.step("vector", f_zsl, reads=[zn, "b2"], writes=["b2"])

            def f_zsq(eng, zG=zG):
                eng.activation(sq_sb[:], zG[:], AF.Square)
            P.step("scalar", f_zsq, reads=[zn], writes=["sq"])

            if nx1:
                emit_rows("mac", True)

            def f_zst(eng):
                ins = None
                for half in range(2):
                    for kt in range(8):
                        eng.matmul(pstat[:, 0:CH], ones_bf[:],
                                   b2h[:, 2048 * half + CH * kt:2048 * half + CH * (kt + 1)],
                                   start=(half == 0 and kt == 0), stop=(half == 1 and kt == 7))
                for kt in range(8):
                    ins = eng.matmul(pstat[:, CH:2 * CH], ones_r[:],
                                     sq_sb[:, CH * kt:CH * (kt + 1)],
                                     start=(kt == 0), stop=(kt == 7))
                return ins
            P.step("tensor", f_zst, reads=["b2", "sq", "ones"], writes=["pstat"])

            if nx1:
                emit_bc(False, a2_sb, "a2_sb")
                mk_corr(W2F, [0, 1, 2, 3], 0)
                mk_main(x_rhs, wffn_wts, [4, 5, 6, 7], CH)
                mk_corr(W2F, [4, 5, 6, 7], CH)

            emit_rows("att", False, need_corr=False)
            emit_bc(True, mu_lng=True)

            def f_znm(eng, zG=zG):
                eng.tensor_tensor(v8(zG), v8(zG), bcast8(pmb[:, 0:CH]), OP.add)
                eng.drain()
                eng.tensor_tensor(v8(zG), v8(zG), bcast8(prb[:, 0:CH]), OP.mult)
                eng.drain()
            P.step("vector", f_znm, reads=[zn, "prb", "pmb"], writes=[zn])

            def f_sg(eng, zG=zG):
                for kt in range(8):
                    eng.activation(sl(zG, kt), sl(zG, kt), AF.Sigmoid,
                                   bias=ccol("lngb", kt), scale=ccol("lngw", kt))
            P.step("scalar", f_sg, reads=[zn, "cols"], writes=[zn])

            def f_gate(eng, resA=resA, zG=zG):
                eng.tensor_tensor(b5[:], zG[:], b5[:], OP.mult)
                eng.drain()
                eng.tensor_tensor(resA[:], resA[:], b5[:], OP.add)
                eng.drain()
            P.step("vector", f_gate, reads=["b5", zn, rn],
                   writes=[rn, "b5"])

            if nx1:
                mk_evac(b2, "b2", [0, 1, 2, 3], 0, a2_sb, "a2_sb")
                mk_evac(b2, "b2", [4, 5, 6, 7], CH, a2_sb, "a2_sb")

                def f_ad1n(eng, zG=zG):
                    eng.tensor_tensor(zG[:], b3[:], b2[:], OP.add)
                    eng.drain()
                P.step("vector", f_ad1n, reads=["b3", "b2", zn], writes=[zn])

            emit_split(resA, rn, spBC, "spBC")
            f2_rhs = [("spBC", sp_hi(spBC, k), sp_lo(spBC, k)) for k in range(8)]
            emit_stats("spBC", spBC)
            emit_rows("mac", True)
            mk_main(f2_rhs, wffn_wts, [0, 1, 2, 3], 0)
            emit_bc(False)
            if nx1:
                emit_split(zG, zn, spA, "spA")
            mk_corr(W2F, [0, 1, 2, 3], 0)
            mk_main(f2_rhs, wffn_wts, [4, 5, 6, 7], CH)
            mk_corr(W2F, [4, 5, 6, 7], CH)
            mk_evac(b2, "b2", [0, 1, 2, 3], 0)
            mk_evac(b2, "b2", [4, 5, 6, 7], CH)

            def f_y3(eng, resA=resA):
                eng.tensor_tensor(resA[:], resA[:], b2[:], OP.add)
                eng.drain()
            P.step("vector", f_y3, reads=["b2", rn], writes=[rn])

            def f_ys(eng, c=c, resA=resA):
                eng.dma_start(out=yv[c],
                              in_=resA.ap().rearrange("p (kt s) -> p kt s", s=CH)
                              ).then_inc(q_y, 16)
            last_ystore = P.step("gpsimd", f_ys, reads=[rn], dma_sem="q_y")

        P.step("gpsimd", lambda eng: None, extra=[last_ystore])

        P.finalize()

        @block.sync
        def _(sync):
            P.emit("sync", sync, sems, qsems)

        @block.gpsimd
        def _(gp):
            P.emit("gpsimd", gp, sems, qsems)

        @block.tensor
        def _(tensor):
            P.emit("tensor", tensor, sems, qsems)

        @block.vector
        def _(vector):
            P.emit("vector", vector, sems, qsems)

        @block.scalar
        def _(scalar):
            P.emit("scalar", scalar, sems, qsems)

    return nc


def host_constants():
    Pm = _sigma()
    pis = _pi_consts()
    t32 = np.linspace(2.0, 8.0, 1024, dtype=np.float32)
    dec32 = (np.float32(0.5) ** (np.float32(1.0) / t32)).astype(np.float32)
    beta32 = (np.float32(1.0) - dec32).astype(np.float32)
    return Pm, pis, dec32, beta32


def kernel(L0_attn_x, L0_attn_Wt, L0_attn_Wo, L0_attn_Wg, L0_attn_bg,
           L0_attn_ln1_w, L0_attn_ln1_b, L0_attn_lnc_w, L0_attn_lnc_b,
           L0_attn_lng_w, L0_attn_lng_b, L0_ffn_G1_R, L0_ffn_G2_R,
           L0_ffn_G1_I, L0_ffn_G2_I, L0_ffn_bias_real, L0_ffn_bias_imag):
    Pm, pis, dec32, beta32 = host_constants()
    f32 = np.float32

    x = np.asarray(L0_attn_x, dtype=f32)
    WtP = np.ascontiguousarray(np.asarray(L0_attn_Wt, dtype=f32)[Pm, :])
    WoP = np.ascontiguousarray(np.asarray(L0_attn_Wo, dtype=f32)[:, Pm])
    Pg = np.concatenate([Pm, 1024 + Pm])
    WgP = np.ascontiguousarray(np.asarray(L0_attn_Wg, dtype=f32)[Pg][:, Pm])

    biasf = np.concatenate([np.asarray(L0_ffn_bias_real, dtype=f32),
                            np.asarray(L0_ffn_bias_imag, dtype=f32)])[Pm]
    bgP = np.asarray(L0_attn_bg, dtype=f32)[Pm]
    rowsc = np.ascontiguousarray(np.stack([biasf, beta32, bgP]).astype(f32))
    cols = np.zeros((128, _COL_W), dtype=f32)

    def put(name, vec):
        cols[:, _COL_OFF[name]:_COL_OFF[name] + 8] = _col_pack(vec)

    put("ln1w", np.asarray(L0_attn_ln1_w, dtype=f32)[Pm])
    put("ln1b", np.asarray(L0_attn_ln1_b, dtype=f32)[Pm])
    put("lngw", np.asarray(L0_attn_lng_w, dtype=f32)[Pm])
    put("lngb", np.asarray(L0_attn_lng_b, dtype=f32)[Pm])
    put("bg", np.asarray(L0_attn_bg, dtype=f32)[Pm])
    put("lncw", np.asarray(L0_attn_lnc_w, dtype=f32))
    put("lncb", np.asarray(L0_attn_lnc_b, dtype=f32))
    put("dec", dec32)

    G1R = np.asarray(L0_ffn_G1_R, dtype=f32)[0]
    G2R = np.asarray(L0_ffn_G2_R, dtype=f32)[:, :, 0, :]
    G1I = np.asarray(L0_ffn_G1_I, dtype=f32)[0]
    G2I = np.asarray(L0_ffn_G2_I, dtype=f32)[:, :, 0, :]
    g2l_r = np.ascontiguousarray(G2R.transpose(0, 2, 1).reshape(4, 256))
    g2l_i = np.ascontiguousarray(G2I.transpose(0, 2, 1).reshape(4, 256))
    g1r_r = np.ascontiguousarray(G1R.transpose(1, 0, 2).reshape(4, 1024))
    g1r_i = np.ascontiguousarray(G1I.transpose(1, 0, 2).reshape(4, 1024))

    nc = build_program()

    in_maps = []
    for i in range(NCORE):
        xT = np.ascontiguousarray(x[i][:, Pm].T)
        in_maps.append({
            "x": xT, "wt": WtP, "wo": WoP, "wg": WgP,
            "cols": cols, "rowsc": rowsc,
            "pis": pis.reshape(8 * 128, 128),
            "g2l_r": g2l_r, "g2l_i": g2l_i, "g1r_r": g1r_r, "g1r_i": g1r_i,
        })

    res = run_bass_kernel_spmd(nc, in_maps, core_ids=list(range(NCORE)))

    out = np.empty((NCORE, SEQ, D), dtype=f32)
    for i in range(NCORE):
        yT = np.asarray(res.results[i]["y"])
        out[i][:, Pm] = yT.T
    return out


# version 24 (final)
# speedup vs baseline: 1.0164x; 1.0164x over previous
import numpy as np
import concourse.bass as bass
import concourse.mybir as mybir
from concourse.bass_utils import run_bass_kernel_spmd

F32 = mybir.dt.float32
F32R = mybir.dt.float32r
BF16 = mybir.dt.bfloat16
AF = mybir.ActivationFunctionType
OP = mybir.AluOpType

D = 1024
SEQ = 2048
NCORE = 8
CH = 256
NCH = SEQ // CH
EPS_MAC = 1e-5
EPS_ATT = 1e-6
INV_D = 1.0 / 1024.0

_COL_NAMES = ["ln1w", "ln1b", "lngw", "lngb", "bg", "lncw", "lncb", "dec"]
_COL_OFF = {n: 8 * i for i, n in enumerate(_COL_NAMES)}
_COL_W = 8 * len(_COL_NAMES)


def _sigma():
    P = np.zeros(D, dtype=np.int64)
    for d_ in range(32):
        for g_ in range(16):
            P[32 * g_ + d_] = 16 * d_ + g_
            P[512 + 32 * g_ + d_] = 512 + 16 * d_ + g_
    return P


def _pi_consts():
    pis = np.zeros((8, 128, 128), dtype=np.float32)
    for neg in range(2):
        for par in range(2):
            for km in range(2):
                j = neg * 4 + par * 2 + km
                for p in range(128):
                    if (p // 32) % 2 != par:
                        continue
                    k = 64 * km + 32 * (p // 64) + (p % 32)
                    pis[j, k, p] = -1.0 if neg else 1.0
    return pis


def _col_pack(vec):
    return np.ascontiguousarray(vec.reshape(8, 128).T.astype(np.float32))


class _Step:
    __slots__ = ("engine", "fn", "deps", "idx", "is_dma", "sem_name", "dma_cum", "order")


class Prog:
    def __init__(self):
        self.steps = []
        self.writer = {}
        self.readers = {}
        self.dma_counts = {}

    def step(self, engine, fn, reads=(), writes=(), extra=(), dma_sem=None, ndma=1):
        st = _Step()
        st.engine = engine
        st.fn = fn
        st.deps = set()
        st.order = len(self.steps)
        for b in reads:
            w = self.writer.get(b)
            if w is not None:
                st.deps.add(w)
        for b in writes:
            w = self.writer.get(b)
            if w is not None:
                st.deps.add(w)
            for r in self.readers.get(b, ()):
                st.deps.add(r)
        for e in extra:
            if e is not None:
                st.deps.add(e)
        st.deps.discard(st)
        for b in reads:
            self.readers.setdefault(b, []).append(st)
        for b in writes:
            self.writer[b] = st
            self.readers[b] = [st]
        st.is_dma = dma_sem is not None
        st.sem_name = dma_sem
        if st.is_dma:
            self.dma_counts[dma_sem] = self.dma_counts.get(dma_sem, 0) + ndma
            st.dma_cum = self.dma_counts[dma_sem]
        for d in st.deps:
            assert d.order < st.order, "dependency on later step"
        self.steps.append(st)
        return st

    def finalize(self):
        idx = {}
        for st in self.steps:
            if not st.is_dma:
                idx[st.engine] = idx.get(st.engine, 0) + 1
                st.idx = idx[st.engine]

    def emit(self, engine, eng, sems, qsems):
        waited = {}
        for st in self.steps:
            if st.engine != engine:
                continue
            for d in sorted(st.deps, key=lambda s: s.order):
                if d.is_dma:
                    key = ("q", d.sem_name)
                    val = 16 * d.dma_cum
                    sem = qsems[d.sem_name]
                else:
                    if d.engine == engine:
                        continue
                    key = ("e", d.engine)
                    val = d.idx
                    sem = sems[d.engine]
                if waited.get(key, 0) < val:
                    eng.wait_ge(sem, val)
                    waited[key] = val
            ret = st.fn(eng)
            if not st.is_dma:
                if engine == "tensor" and ret is not None:
                    ret.then_inc(sems[engine], 1)
                elif engine == "tensor":
                    eng.nop().then_inc(sems[engine], 1)
                else:
                    eng.drain()
                    eng.sem_inc(sems[engine], 1)


def build_program():
    nc = bass.Bass("TRN2", target_bir_lowering=False)

    x_hbm = nc.dram_tensor("x", [D, SEQ], F32, kind="ExternalInput").ap()
    y_hbm = nc.dram_tensor("y", [D, SEQ], F32, kind="ExternalOutput").ap()
    wt_hbm = nc.dram_tensor("wt", [D, D], F32, kind="ExternalInput").ap()
    wo_hbm = nc.dram_tensor("wo", [D, D], F32, kind="ExternalInput").ap()
    wg_hbm = nc.dram_tensor("wg", [2 * D, D], F32, kind="ExternalInput").ap()
    cols_hbm = nc.dram_tensor("cols", [128, _COL_W], F32, kind="ExternalInput").ap()
    rowsc_hbm = nc.dram_tensor("rowsc", [3, D], F32, kind="ExternalInput").ap()
    pis_hbm = nc.dram_tensor("pis", [8 * 128, 128], F32, kind="ExternalInput").ap()
    g2l_r_h = nc.dram_tensor("g2l_r", [4, 256], F32, kind="ExternalInput").ap()
    g2l_i_h = nc.dram_tensor("g2l_i", [4, 256], F32, kind="ExternalInput").ap()
    g1r_r_h = nc.dram_tensor("g1r_r", [4, 1024], F32, kind="ExternalInput").ap()
    g1r_i_h = nc.dram_tensor("g1r_i", [4, 1024], F32, kind="ExternalInput").ap()
    wgspill = nc.dram_tensor("wgspill", [16, 128, 2048], BF16, kind="Internal").ap()

    import contextlib
    es = contextlib.ExitStack()
    with es:
        def sb(name, shape, dt=F32):
            return es.enter_context(nc.sbuf_tensor(name, shape, dt))

        def psum(name, shape):
            return es.enter_context(nc.psum_tensor(name, shape, F32))

        b1 = sb("b1", [128, 2048])
        b2 = sb("b2", [128, 2048])
        b3 = sb("b3", [128, 2048])
        b5 = sb("b5", [128, 2048])
        b6 = sb("b6", [128, 2048])

        wffn_h = sb("wffn_h", [128, 8192], BF16)
        wffn_l = sb("wffn_l", [128, 8192], BF16)
        wt_h = sb("wt_h", [128, 8192], BF16)
        wt_l = sb("wt_l", [128, 8192], BF16)
        wo_h = sb("wo_h", [128, 8192], BF16)
        wo_l = sb("wo_l", [128, 8192], BF16)
        wgs = [sb(f"wgs{i}", [128, 8192], BF16) for i in range(2)]
        spA = sb("spA", [128, 4096], BF16)
        spBC = sb("spBC", [128, 4096], BF16)

        sq_sb = sb("sq_sb", [128, 2048], F32R)
        cols = sb("cols_sb", [128, _COL_W])
        carry = sb("carry", [128, 8])
        ones_bf = sb("ones_bf", [128, 128], BF16)
        ones_r = sb("ones_r", [128, 128], F32R)
        ones_row = sb("ones_row", [1, 128])
        ones_col = sb("ones_col", [128, 1])
        a_sb = sb("a_sb", [128, CH])
        a2_sb = sb("a2_sb", [128, CH])
        r2 = sb("r2", [66, CH])
        r_t = sb("r_t", [66, CH])
        r_a = sb("r_a", [66, CH])
        r_rho = sb("r_rho", [2, CH])
        r_lm = a_sb
        w2all = sb("w2all", [66, 2048])
        W2F, W2T, W2O = 0, 32, 64
        W2G = (0, 1024)
        g2l_r = wffn_h[0:4, 0:512].bitcast(F32)
        g2l_i = wffn_l[0:4, 0:512].bitcast(F32)
        g1r_r = wffn_h[0:4, 512:2560].bitcast(F32)
        g1r_i = wffn_l[0:4, 512:2560].bitcast(F32)

        s_sb = wgs[0][:, 0:4096].bitcast(F32)
        u_r = wgs[0][:, 4096:8192].bitcast(F32)
        u_i = wgs[1][:, 0:4096].bitcast(F32)
        pis_sb = wgs[1][:, 4096:6144].bitcast(F32)

        pmain = [psum(f"pm{i}", [128, 512]) for i in range(4)]
        pstat = psum("pstat", [128, 512])
        prb = psum("prb", [128, 512])
        pmb = psum("pmb", [128, 512])
        pdec = psum("pdec", [128, 512])

        sem_T = es.enter_context(nc.semaphore("s_T"))
        sem_V = es.enter_context(nc.semaphore("s_V"))
        sem_S = es.enter_context(nc.semaphore("s_S"))
        sem_G = es.enter_context(nc.semaphore("s_G"))
        q_c = es.enter_context(nc.semaphore("q_c"))
        q_wA = es.enter_context(nc.semaphore("q_wA"))
        q_wB = es.enter_context(nc.semaphore("q_wB"))
        q_x = es.enter_context(nc.semaphore("q_x"))
        q_y = es.enter_context(nc.semaphore("q_y"))
        q_s = [es.enter_context(nc.semaphore(f"q_s{i}")) for i in range(4)]
        q_r = es.enter_context(nc.semaphore("q_r"))
        block = es.enter_context(nc.Block())

        sems = {"tensor": sem_T, "vector": sem_V, "scalar": sem_S, "gpsimd": sem_G}
        qsems = {"q_c": q_c, "q_wA": q_wA, "q_wB": q_wB, "q_x": q_x, "q_y": q_y, "q_r": q_r,
                 "q_s0": q_s[0], "q_s1": q_s[1], "q_s2": q_s[2], "q_s3": q_s[3]}

        P = Prog()

        def sl(slab, kt, w=CH):
            return slab[:, w * kt:w * (kt + 1)]

        def ccol(name, kt):
            o = _COL_OFF[name] + kt
            return cols[:, o:o + 1]

        def bcast8(prow):
            return prow.unsqueeze(1).broadcast_to((128, 8, CH))

        def v8(slab):
            return slab.ap().rearrange("p (c s) -> p c s", s=CH)

        def f_consts(eng):
            eng.dma_start(out=cols[:], in_=cols_hbm).then_inc(q_c, 16)
            eng.dma_start(out=pis_sb.rearrange("p (j k) -> p j k", j=8),
                          in_=pis_hbm.rearrange("(j p) k -> p j k", p=128)).then_inc(q_c, 16)
            eng.dma_start(out=g2l_r[:], in_=g2l_r_h).then_inc(q_c, 16)
            eng.dma_start(out=g2l_i[:], in_=g2l_i_h).then_inc(q_c, 16)
            eng.dma_start(out=g1r_r[:], in_=g1r_r_h).then_inc(q_c, 16)
            eng.dma_start(out=g1r_i[:], in_=g1r_i_h).then_inc(q_c, 16)
            eng.dma_start(out=w2all[W2F + 1:W2F + 2, 0:1024], in_=rowsc_hbm[0:1, :]).then_inc(q_c, 16)
            eng.dma_start(out=b2[0:1, 1024:2048], in_=rowsc_hbm[1:2, :]).then_inc(q_c, 16)
        P.step("sync", f_consts, writes=["cols", "wgs1", "wffn_h", "wffn_l", "w2all", "b2"],
               dma_sem="q_c", ndma=8)

        def f_init(eng):
            eng.memset(ones_bf[:], 1.0)
            eng.memset(ones_row[:], 1.0)
            eng.memset(ones_col[:], 1.0)
            eng.memset(carry[:], 0.0)
            eng.memset(pdec[:], 1.0)
            eng.drain()
            eng.tensor_copy(ones_r[:].bitcast(F32R), ones_bf[:])
            eng.drain()
        P.step("vector", f_init, writes=["ones", "carry", "pdec"])

        def f_dec(eng):
            for g in range(8):
                eng.tensor_scalar(pdec[:, 64 * g:64 * (g + 1)],
                                  pdec[:, 64 * g:64 * (g + 1)],
                                  ccol("dec", g), None, OP.mult)
            eng.drain()
        P.step("vector", f_dec, reads=["cols", "pdec"], writes=["pdec"])

        for ri, (g2l_t, g1r_t, u_t, uname) in enumerate(
                [(g2l_r, g1r_r, u_r, "wgs0"), (g2l_i, g1r_i, u_i, "wgs1")]):
            for t in range(2):
                for n in range(2):
                    pb = pmain[(t * 2 + n) % 2]
                    pname = f"pm{(t * 2 + n) % 2}"

                    def f_smm(eng, g2l_t=g2l_t, g1r_t=g1r_t, t=t, n=n, pb=pb):
                        return eng.matmul(pb[:, 0:512], g2l_t[:, 128 * t:128 * (t + 1)],
                                          g1r_t[:, 512 * n:512 * (n + 1)],
                                          start=True, stop=True)
                    P.step("tensor", f_smm, reads=["wffn_h", "wffn_l"], writes=[pname])

                    def f_sev(eng, t=t, n=n, pb=pb):
                        eng.copy(s_sb[:, 1024 * t + 512 * n:1024 * t + 512 * n + 512],
                                 pb[:, 0:512])
                    P.step("scalar", f_sev, reads=[pname], writes=["wgs0"])

            def f_tr(eng, u_t=u_t):
                eng.transpose(u_t[:, 0:1024], s_sb[:, 0:1024])
                eng.transpose(u_t[:, 1024:2048], s_sb[:, 1024:2048])
                eng.drain()
            P.step("vector", f_tr, reads=["wgs0"], writes=[uname])

        for kt in range(8):
            ktl = kt % 4
            tu = ktl // 2
            km = ktl % 2
            for half in range(2):
                if kt < 4:
                    u_t, uname, neg = (u_r, "wgs0", 0) if half == 0 else (u_i, "wgs1", 0)
                else:
                    u_t, uname, neg = (u_i, "wgs1", 1) if half == 0 else (u_r, "wgs0", 0)
                bi = 2 + (2 * kt + half) % 2
                pb, pbn = pmain[bi], f"pm{bi}"

                def f_pmm(eng, u_t=u_t, tu=tu, km=km, neg=neg, pb=pb):
                    uview = u_t[:, 1024 * tu:1024 * (tu + 1)].rearrange(
                        "p (b two e) -> p two e b", b=32, two=2, e=16)
                    ins = None
                    for par in range(2):
                        j = neg * 4 + par * 2 + km
                        ins = eng.matmul(pb[:, 0:512], pis_sb[:, 128 * j:128 * (j + 1)],
                                         uview[:, par, :, :], start=(par == 0), stop=(par == 1))
                    return ins
                P.step("tensor", f_pmm, reads=[uname, "wgs1"], writes=[pbn])

                def f_pevh(eng, kt=kt, half=half, pb=pb):
                    o = 1024 * kt + 512 * half
                    eng.activation(wffn_h[:, o:o + 512], pb[:, 0:512], AF.Copy)
                P.step("scalar", f_pevh, reads=[pbn], writes=["wffn_h"])

                def f_pevl(eng, kt=kt, half=half, pb=pb):
                    o = 1024 * kt + 512 * half
                    eng.tensor_tensor(wffn_l[:, o:o + 512], pb[:, 0:512],
                                      wffn_h[:, o:o + 512], OP.subtract)
                    eng.drain()
                P.step("vector", f_pevl, reads=[pbn, "wffn_h"], writes=["wffn_l"])

        def f_w1f(eng):
            ins = None
            for half in range(2):
                pb = pmain[half]
                for kt in range(8):
                    eng.matmul(pb[0:1, 0:512], ones_bf[:, 0:1],
                               wffn_h[:, 1024 * kt + 512 * half:1024 * kt + 512 * (half + 1)],
                               start=(kt == 0), stop=False)
                    ins = eng.matmul(pb[0:1, 0:512], ones_bf[:, 0:1],
                                     wffn_l[:, 1024 * kt + 512 * half:1024 * kt + 512 * (half + 1)],
                                     start=False, stop=(kt == 7))
            return ins
        P.step("tensor", f_w1f, reads=["wffn_h", "wffn_l", "ones"],
               writes=["pm0", "pm1"])

        def f_w1fev(eng):
            eng.activation(w2all[W2F:W2F + 1, 0:512], pmain[0][0:1, 0:512], AF.Copy)
            eng.activation(w2all[W2F:W2F + 1, 512:1024], pmain[1][0:1, 0:512], AF.Copy)
        P.step("scalar", f_w1fev, reads=["pm0", "pm1"], writes=["w2all"])

        wt_v = wt_hbm.rearrange("(kt p) m -> kt p m", p=128)
        wo_v = wo_hbm.rearrange("(kt p) m -> kt p m", p=128)
        wg_v = wg_hbm.rearrange("(kt p) m -> kt p m", p=128)

        stage_bufs = [(b3, "b3"), (b5, "b5"), (b6, "b6"), (b1, "b1")]

        def f_bbc(eng):
            eng.matmul(pmain[2][:, 0:512], ones_row[0:1, :], b2[0:1, 1024:1536],
                       start=True, stop=True)
            return eng.matmul(pmain[3][:, 0:512], ones_row[0:1, :], b2[0:1, 1536:2048],
                              start=True, stop=True)
        P.step("tensor", f_bbc, reads=["b2", "ones"], writes=["pm2", "pm3"])

        def emit_wsplit(tag, src_v, kts, dsth, dsthname, dstl, dstlname,
                        scale_col, bias_col, beta_scale, rot0=0):
            for i0 in range(0, len(kts), 2):
                pair = kts[i0:i0 + 2]
                ri = (rot0 + i0 // 2) % 4
                stg, sn = stage_bufs[ri]
                qs = f"q_s{ri}"

                def f_ld(eng, pair=pair, stg=stg, qs=qs):
                    for j, kt in enumerate(pair):
                        eng.dma_start(out=stg[:, 1024 * j:1024 * (j + 1)],
                                      in_=src_v[kt]).then_inc(qsems[qs], 16)
                P.step("sync", f_ld, writes=[sn], dma_sem=qs, ndma=len(pair))

                if beta_scale:
                    def f_bs(eng, pair=pair, stg=stg):
                        for j in range(len(pair)):
                            eng.tensor_tensor(stg[:, 1024 * j:1024 * j + 512],
                                              stg[:, 1024 * j:1024 * j + 512],
                                              pmain[2][:, 0:512], OP.mult)
                            eng.tensor_tensor(stg[:, 1024 * j + 512:1024 * (j + 1)],
                                              stg[:, 1024 * j + 512:1024 * (j + 1)],
                                              pmain[3][:, 0:512], OP.mult)
                        eng.drain()
                    P.step("vector", f_bs, reads=[sn, "pm2", "pm3"], writes=[sn])

                def f_brow(eng, pair=pair, stg=stg):
                    ins = None
                    for j, kt in enumerate(pair):
                        eng.matmul(pstat[0:1, 0:512], ccol(bias_col, kt),
                                   stg[:, 1024 * j:1024 * j + 512],
                                   start=(kt == 0), stop=(kt == 7))
                        ins = eng.matmul(prb[0:1, 0:512], ccol(bias_col, kt),
                                         stg[:, 1024 * j + 512:1024 * (j + 1)],
                                         start=(kt == 0), stop=(kt == 7))
                    return ins
                P.step("tensor", f_brow, reads=[sn, "cols"], writes=["pstat", "prb"])

                def f_sc(eng, pair=pair, stg=stg):
                    for j, kt in enumerate(pair):
                        eng.tensor_scalar(stg[:, 1024 * j:1024 * (j + 1)],
                                          stg[:, 1024 * j:1024 * (j + 1)],
                                          ccol(scale_col, kt), None, OP.mult)
                    eng.drain()
                P.step("vector", f_sc, reads=[sn, "cols"], writes=[sn])

                def f_w1(eng, pair=pair, stg=stg):
                    ins = None
                    for j, kt in enumerate(pair):
                        eng.matmul(pmain[0][0:1, 0:512], ones_col[:],
                                   stg[:, 1024 * j:1024 * j + 512],
                                   start=(kt == 0), stop=(kt == 7))
                        ins = eng.matmul(pmain[1][0:1, 0:512], ones_col[:],
                                         stg[:, 1024 * j + 512:1024 * (j + 1)],
                                         start=(kt == 0), stop=(kt == 7))
                    return ins
                P.step("tensor", f_w1, reads=[sn, "ones"], writes=["pm0", "pm1"])

                def f_hi(eng, pair=pair, stg=stg):
                    for j, kt in enumerate(pair):
                        eng.activation(sl(dsth, kt, 1024),
                                       stg[:, 1024 * j:1024 * (j + 1)], AF.Copy)
                P.step("scalar", f_hi, reads=[sn], writes=[dsthname])

                def f_lo(eng, pair=pair, stg=stg):
                    for j, kt in enumerate(pair):
                        eng.tensor_tensor(sl(dstl, kt, 1024),
                                          stg[:, 1024 * j:1024 * (j + 1)],
                                          sl(dsth, kt, 1024), OP.subtract)
                    eng.drain()
                P.step("vector", f_lo, reads=[sn, dsthname], writes=[dstlname])

        def emit_wrows(base):
            def f_ev(eng):
                eng.activation(w2all[base:base + 1, 0:512], pmain[0][0:1, 0:512], AF.Copy)
                eng.activation(w2all[base:base + 1, 512:1024], pmain[1][0:1, 0:512], AF.Copy)
                eng.activation(b2[0:1, 0:512], pstat[0:1, 0:512], AF.Copy)
                eng.activation(b2[0:1, 512:1024], prb[0:1, 0:512], AF.Copy)
            P.step("scalar", f_ev, reads=["pm0", "pm1", "pstat", "prb"],
                   writes=["w2all", "b2"])

            def f_dma(eng):
                eng.dma_start(out=w2all[base + 1:base + 2, 0:1024],
                              in_=b2[0:1, 0:1024]).then_inc(q_c, 16)
            P.step("sync", f_dma, reads=["b2"], writes=["w2all"], dma_sem="q_c")

        emit_wsplit("wt", wt_v, list(range(8)), wt_h, "wt_h", wt_l, "wt_l",
                    scale_col="ln1w", bias_col="ln1b", beta_scale=True, rot0=0)
        emit_wrows(W2T)
        emit_wsplit("wo", wo_v, list(range(8)), wo_h, "wo_h", wo_l, "wo_l",
                    scale_col="lncw", bias_col="lncb", beta_scale=False, rot0=0)
        emit_wrows(W2O)

        def f_bgdma(eng):
            eng.dma_start(out=b2[0:1, 1024:2048],
                          in_=rowsc_hbm[2:3, :]).then_inc(q_c, 16)
        P.step("sync", f_bgdma, writes=["b2"], dma_sem="q_c")

        wg_stage = [(b3, "b3"), (b5, "b5")]
        wg_out = [(spBC, "spBC"), (spA, "spA")]

        def emit_wg_pair(i0):
            pair = [i0, i0 + 1]
            ri = (i0 // 2) % 2
            stg, sn = wg_stage[ri]
            out, on = wg_out[ri]
            qs = f"q_s{ri}"

            def f_ld(eng, pair=pair, stg=stg, qs=qs):
                for j, kt in enumerate(pair):
                    eng.dma_start(out=stg[:, 1024 * j:1024 * (j + 1)],
                                  in_=wg_v[kt]).then_inc(qsems[qs], 16)
            P.step("sync", f_ld, writes=[sn], dma_sem=qs, ndma=2)

            if i0 < 8:
                def f_gbrow(eng, pair=pair, stg=stg):
                    ins = None
                    for j, kt in enumerate(pair):
                        eng.matmul(pstat[0:1, 0:512], ccol("ln1b", kt),
                                   stg[:, 1024 * j:1024 * j + 512],
                                   start=(kt == 0), stop=(kt == 7))
                        ins = eng.matmul(prb[0:1, 0:512], ccol("ln1b", kt),
                                         stg[:, 1024 * j + 512:1024 * (j + 1)],
                                         start=(kt == 0), stop=(kt == 7))
                    return ins
                P.step("tensor", f_gbrow, reads=[sn, "cols"], writes=["pstat", "prb"])

                def f_gsc(eng, pair=pair, stg=stg):
                    for j, kt in enumerate(pair):
                        eng.tensor_scalar(stg[:, 1024 * j:1024 * (j + 1)],
                                          stg[:, 1024 * j:1024 * (j + 1)],
                                          ccol("ln1w", kt), None, OP.mult)
                    eng.drain()
                P.step("vector", f_gsc, reads=[sn, "cols"], writes=[sn])

                def f_gw1(eng, pair=pair, stg=stg):
                    ins = None
                    for j, kt in enumerate(pair):
                        eng.matmul(pmain[0][0:1, 0:512], ones_col[:],
                                   stg[:, 1024 * j:1024 * j + 512],
                                   start=(kt == 0), stop=(kt == 7))
                        ins = eng.matmul(pmain[1][0:1, 0:512], ones_col[:],
                                         stg[:, 1024 * j + 512:1024 * (j + 1)],
                                         start=(kt == 0), stop=(kt == 7))
                    return ins
                P.step("tensor", f_gw1, reads=[sn, "ones"], writes=["pm0", "pm1"])

            def f_hi(eng, pair=pair, stg=stg, out=out):
                for j, kt in enumerate(pair):
                    eng.activation(out[:, 2048 * j:2048 * j + 1024],
                                   stg[:, 1024 * j:1024 * (j + 1)], AF.Copy)
            P.step("scalar", f_hi, reads=[sn], writes=[on])

            def f_lo(eng, pair=pair, stg=stg, out=out):
                for j, kt in enumerate(pair):
                    eng.tensor_tensor(out[:, 2048 * j + 1024:2048 * (j + 1)],
                                      stg[:, 1024 * j:1024 * (j + 1)],
                                      out[:, 2048 * j:2048 * j + 1024], OP.subtract)
                eng.drain()
            P.step("vector", f_lo, reads=[sn, on], writes=[on])

            def f_sp(eng, pair=pair, out=out):
                for j, kt in enumerate(pair):
                    eng.dma_start(out=wgspill[kt],
                                  in_=out[:, 2048 * j:2048 * (j + 1)]).then_inc(q_wB, 16)
            P.step("sync", f_sp, reads=[on], writes=["wgspill"],
                   dma_sem="q_wB", ndma=2)

        for i0 in range(0, 8, 2):
            emit_wg_pair(i0)

        def f_gwev(eng):
            eng.activation(w2all[0:1, 1024:1536], pmain[0][0:1, 0:512], AF.Copy)
            eng.activation(w2all[0:1, 1536:2048], pmain[1][0:1, 0:512], AF.Copy)
            eng.activation(b2[0:1, 0:512], pstat[0:1, 0:512], AF.Copy)
            eng.activation(b2[0:1, 512:1024], prb[0:1, 0:512], AF.Copy)
        P.step("scalar", f_gwev, reads=["pm0", "pm1", "pstat", "prb"],
               writes=["w2all", "b2"])

        def f_bgadd(eng):
            eng.tensor_tensor(b2[0:1, 0:1024], b2[0:1, 0:1024],
                              b2[0:1, 1024:2048], OP.add)
            eng.drain()
        P.step("vector", f_bgadd, reads=["b2"], writes=["b2"])

        def f_gwdma(eng):
            eng.dma_start(out=w2all[1:2, 1024:2048],
                          in_=b2[0:1, 0:1024]).then_inc(q_c, 16)
        P.step("sync", f_gwdma, reads=["b2"], writes=["w2all"], dma_sem="q_c")

        def sp_hi(sp, kt):
            return sp[:, CH * kt:CH * (kt + 1)]

        def sp_lo(sp, kt):
            return sp[:, 2048 + CH * kt:2048 + CH * (kt + 1)]

        def wtile(buf, kt):
            return buf[:, 1024 * kt:1024 * (kt + 1)]

        def wg_w(kt):
            buf = wgs[(kt // 4) % 2]
            lk = kt % 4
            return (f"wgs{(kt // 4) % 2}",
                    buf[:, 2048 * lk:2048 * lk + 1024],
                    buf[:, 2048 * lk + 1024:2048 * (lk + 1)])

        def emit_split(src, srcname, sp, spname, with_sq=True):
            def f_hi(eng):
                eng.activation(sp[:, 0:2048], src[:], AF.Copy)
            P.step("scalar", f_hi, reads=[srcname], writes=[spname])

            def f_lo(eng):
                eng.tensor_tensor(sp[:, 2048:4096], src[:], sp[:, 0:2048], OP.subtract)
                eng.drain()
            P.step("vector", f_lo, reads=[srcname, spname], writes=[spname])

            if with_sq:
                def f_sq(eng):
                    eng.activation(sq_sb[:], src[:], AF.Square)
                P.step("scalar", f_sq, reads=[srcname], writes=["sq"])

        def emit_stats(spname, sp):
            def f_st(eng):
                ins = None
                for half in range(2):
                    for kt in range(8):
                        eng.matmul(pstat[:, 0:CH], ones_bf[:],
                                   sp[:, 2048 * half + CH * kt:2048 * half + CH * (kt + 1)],
                                   start=(half == 0 and kt == 0), stop=(half == 1 and kt == 7))
                for kt in range(8):
                    ins = eng.matmul(pstat[:, CH:2 * CH], ones_r[:],
                                     sq_sb[:, CH * kt:CH * (kt + 1)],
                                     start=(kt == 0), stop=(kt == 7))
                return ins
            P.step("tensor", f_st, reads=[spname, "sq", "ones"], writes=["pstat"])

        def emit_rows(kind, half, need_corr=True):
            nl = 66 if need_corr else 1
            mu = r2 if need_corr else r_lm

            def f_row1(eng):
                eng.tensor_scalar(mu[0:nl, :], pstat[0:nl, 0:CH], -INV_D, None, OP.mult)
                eng.drain()
                eng.tensor_tensor(r_t[0:nl, :], mu[0:nl, :], mu[0:nl, :], OP.mult)
                eng.drain()
                eng.scalar_tensor_tensor(r_a[0:nl, :], pstat[0:nl, CH:2 * CH], INV_D,
                                         r_t[0:nl, :], OP.mult, OP.subtract)
                eng.drain()
                if kind == "fused":
                    eng.tensor_scalar(r_a[0:nl, :], r_a[0:nl, :], 1.0 + EPS_ATT,
                                      EPS_MAC * EPS_ATT, OP.mult, OP.add)
                    eng.drain()
                else:
                    eps = EPS_MAC if kind == "mac" else EPS_ATT
                    eng.tensor_scalar(r_a[0:nl, :], r_a[0:nl, :], eps, None, OP.add)
                    eng.drain()
            P.step("vector", f_row1, reads=["pstat"],
                   writes=(["r2", "rows", "rows2"] if need_corr
                           else ["a_sb", "rows", "rows2"]))

            def f_sqrt(eng):
                eng.activation(r_t[0:nl, :], r_a[0:nl, :], AF.Sqrt)
                if not need_corr:
                    eng.drain()
                    eng.activation(r_a[0:1, 0:1], r_a[0:1, 0:1], AF.Sigmoid)
            P.step("scalar", f_sqrt, reads=["rows"],
                   writes=(["rows2"] if need_corr else ["rows2", "rows"]))

            def f_row2(eng):
                eng.reciprocal(r_rho[0:1, :], r_t[0:1, :])
                eng.drain()
                if half:
                    eng.tensor_scalar(r_rho[0:1, :], r_rho[0:1, :], 0.5, None, OP.mult)
                    eng.drain()
            P.step("vector", f_row2, reads=["rows2"], writes=["rows3"])

            if need_corr:
                def f_rdma(eng):
                    for i in (1, 33, 65):
                        eng.dma_start(out=r2[i:i + 1, :],
                                      in_=r_t[0:1, :]).then_inc(q_r, 16)
                P.step("sync", f_rdma, reads=["rows2", "r2"], writes=["r2"],
                       dma_sem="q_r", ndma=3)

        def emit_bc(negmu_bc, ab=None, abname=None, mu_lng=False):
            if ab is None:
                ab, abname = a_sb, "a_sb"
            mu, mun = (r_lm, "a_sb") if mu_lng else (r2, "r2")

            def f_bc(eng):
                ins = eng.matmul(prb[:, 0:CH], ones_row[0:1, :], r_rho[0:1, :],
                                 start=True, stop=True)
                if negmu_bc:
                    ins = eng.matmul(pmb[:, 0:CH], ones_row[0:1, :], mu[0:1, :],
                                     start=True, stop=True)
                return ins
            P.step("tensor", f_bc, reads=["rows3", mun, "ones"],
                   writes=["prb"] + (["pmb"] if negmu_bc else []))

            def f_acp(eng):
                eng.activation(ab[:], prb[:, 0:CH], AF.Copy)
            P.step("scalar", f_acp, reads=["prb"], writes=[abname])

        def mk_main(rhs, wts, mts, reg, close=False, ks=None, opens=True):
            if ks is None:
                ks = list(range(8))
            rnames = sorted({n for n, _, _ in (rhs[k] for k in ks)})
            wnames = sorted({n for n, _, _ in (wts[k] for k in ks)})

            def f_mm(eng):
                ins = None
                first = True
                for pas in range(3):
                    for i in ks:
                        _, xh, xl = rhs[i]
                        _, wh, wl = wts[i]
                        xap = xh if pas in (0, 1) else xl
                        wsel = wh if pas in (0, 2) else wl
                        for mt in mts:
                            ins = eng.matmul(
                                pmain[mt % 4][:, reg:reg + CH],
                                wsel[:, 128 * mt:128 * (mt + 1)], xap,
                                start=(opens and pas == 0 and first),
                                stop=(close and pas == 2 and i == ks[-1]))
                        first = False
                return ins
            P.step("tensor", f_mm, reads=rnames + wnames,
                   writes=[f"pm{m % 4}" for m in mts])

        def mk_corr(base, mts, reg, coff=0):
            def f_c(eng):
                ins = None
                for mt in mts:
                    ins = eng.matmul(pmain[mt % 4][:, reg:reg + CH],
                                     w2all[base:base + 2,
                                           coff + 128 * mt:coff + 128 * (mt + 1)],
                                     r2[base:base + 2, 0:CH], start=False, stop=True)
                return ins
            P.step("tensor", f_c, reads=["r2", "w2all"],
                   writes=[f"pm{m % 4}" for m in mts])

        def mk_evac(dst, dstname, mts, reg, ab=None, abname=None):
            if ab is None:
                ab, abname = a_sb, "a_sb"

            def f_e(eng):
                for mt in mts:
                    eng.tensor_tensor(sl(dst, mt), pmain[mt % 4][:, reg:reg + CH],
                                      ab[:], OP.mult)
                eng.drain()
            P.step("vector", f_e,
                   reads=[f"pm{m % 4}" for m in mts] + [abname], writes=[dstname])

        def emit_site_mms(sp, spname, wts, base, dst, dstname,
                          kind, half, negmu_bc=False, ab=None, abname=None):
            rhs = [(spname, sp_hi(sp, k), sp_lo(sp, k)) for k in range(8)]
            emit_stats(spname, sp)
            emit_rows(kind, half)
            mk_main(rhs, wts, [0, 1, 2, 3], 0)
            emit_bc(negmu_bc, ab, abname)
            mk_corr(base, [0, 1, 2, 3], 0)
            mk_main(rhs, wts, [4, 5, 6, 7], CH)
            mk_corr(base, [4, 5, 6, 7], CH)
            mk_evac(dst, dstname, [0, 1, 2, 3], 0, ab, abname)
            mk_evac(dst, dstname, [4, 5, 6, 7], CH, ab, abname)

        xv = x_hbm.rearrange("(kt p) (c s) -> c p kt s", p=128, s=CH)
        yv = y_hbm.rearrange("(kt p) (c s) -> c p kt s", p=128, s=CH)

        wffn_wts = [("wffn_h", wtile(wffn_h, k), wtile(wffn_l, k)) for k in range(8)]
        wt_wts = [("wt_h", wtile(wt_h, k), wtile(wt_l, k)) for k in range(8)]
        wo_wts = [("wo_h", wtile(wo_h, k), wtile(wo_l, k)) for k in range(8)]

        def emit_wgl(q):
            def f_wgl(eng, q=q):
                eng.dma_start(
                    out=wgs[q % 2].ap().rearrange("p (kt m) -> p kt m", m=2048),
                    in_=wgspill[4 * q:4 * (q + 1)].rearrange("kt p m -> p kt m")
                ).then_inc(q_wB if q % 2 else q_wA, 16)
            P.step("sync", f_wgl, reads=["wgspill"], writes=[f"wgs{q % 2}"],
                   dma_sem=("q_wB" if q % 2 else "q_wA"))

        last_ystore = None
        for c in range(NCH):
            resA, rn = (b1, "b1") if c % 2 == 0 else (b6, "b6")
            zG, zn = (b6, "b6") if c % 2 == 0 else (b1, "b1")

            emit_wgl(0)
            emit_wgl(1)

            if c == 0:
                def f_xl0(eng):
                    eng.dma_start(out=b3.ap().rearrange("p (kt s) -> p kt s", s=CH),
                                  in_=xv[0]).then_inc(q_x, 16)
                P.step("gpsimd", f_xl0, writes=["b3"], dma_sem="q_x")
                emit_split(b3, "b3", spA, "spA")
                emit_site_mms(spA, "spA", wffn_wts, W2F, b5, "b5", "mac", True,
                              ab=a2_sb, abname="a2_sb")

            if c == 0:
                def f_ad1(eng, resA=resA):
                    eng.tensor_tensor(resA[:], b3[:], b5[:], OP.add)
                    eng.drain()
                P.step("vector", f_ad1, reads=["b3", "b5"], writes=[rn])
                emit_wg_pair(8)
                emit_wg_pair(10)
                emit_split(resA, rn, spA, "spA")

            emit_site_mms(spA, "spA", wt_wts, W2T, b2, "b2",
                          "fused", False, negmu_bc=True, ab=a2_sb, abname="a2_sb")
            if c == 0:
                emit_wg_pair(12)

            def f_scan(eng):
                for sub in range(4):
                    for g in range(8):
                        o = CH * g + 64 * sub
                        init = carry[:, g:g + 1] if sub == 0 else b2[:, o - 1:o]
                        eng.tensor_tensor_scan(b2[:, o:o + 64],
                                               pdec[:, 64 * g:64 * (g + 1)],
                                               b2[:, o:o + 64],
                                               init, OP.mult, OP.add)
                    eng.drain()
                for g in range(8):
                    eng.tensor_copy(carry[:, g:g + 1], b2[:, CH * g + CH - 1:CH * g + CH])
                eng.drain()
            P.step("vector", f_scan, reads=["b2", "pdec", "carry"],
                   writes=["b2", "carry"])

            wga_rhs = [("spA", sp_hi(spA, k), sp_lo(spA, k)) for k in range(8)]
            wga_wts = [wg_w(k) for k in range(8)]
            mk_main(wga_rhs, wga_wts, [0, 1, 2, 3], 0)
            mk_corr(W2G[0], [0, 1, 2, 3], 0, coff=W2G[1])
            mk_main(wga_rhs, wga_wts, [4, 5, 6, 7], CH)
            mk_corr(W2G[0], [4, 5, 6, 7], CH, coff=W2G[1])
            emit_split(b2, "b2", spBC, "spBC")
            mk_evac(zG, zn, [0, 1, 2, 3], 0, a2_sb, "a2_sb")
            mk_evac(zG, zn, [4, 5, 6, 7], CH, a2_sb, "a2_sb")
            if c == 0:
                emit_wg_pair(14)

            emit_wgl(2)
            emit_wgl(3)

            def f_xm(eng, resA=resA):
                eng.tensor_tensor(v8(b3), v8(resA), bcast8(pmb[:, 0:CH]), OP.add)
                eng.drain()
                eng.tensor_tensor(v8(b3), v8(b3),
                                  a2_sb[:].unsqueeze(1).broadcast_to((128, 8, CH)),
                                  OP.mult)
                eng.drain()
            P.step("vector", f_xm, reads=[rn, "a2_sb", "pmb"], writes=["b3"])

            def f_xaf(eng):
                for kt in range(8):
                    eng.activation(sl(b3, kt), sl(b3, kt), AF.Identity,
                                   bias=ccol("ln1b", kt), scale=ccol("ln1w", kt))
            P.step("scalar", f_xaf, reads=["b3", "cols"], writes=["b3"])

            emit_site_mms(spBC, "spBC", wo_wts, W2O, b5, "b5", "lnc", False)

            emit_split(b5, "b5", spA, "spA", with_sq=False)

            def f_pregate(eng, resA=resA):
                eng.tensor_tensor(b5[:], b5[:], b3[:], OP.subtract)
                eng.drain()
                eng.tensor_tensor(resA[:], resA[:], b3[:], OP.add)
                eng.drain()
            P.step("vector", f_pregate, reads=["b5", "b3", rn], writes=["b5", rn])

            nx1 = c < NCH - 1
            if nx1:
                def f_xl(eng, c=c):
                    eng.dma_start(out=b3.ap().rearrange("p (kt s) -> p kt s", s=CH),
                                  in_=xv[c + 1]).then_inc(q_x, 16)
                P.step("gpsimd", f_xl, writes=["b3"], dma_sem="q_x")
                emit_split(b3, "b3", spBC, "spBC")

            wgb_rhs = [("spA", sp_hi(spA, k), sp_lo(spA, k)) for k in range(8)]
            wgb_wts = [wg_w(8 + k) for k in range(8)]
            mk_main(wgb_rhs, wgb_wts, [0, 1, 2, 3], 0, ks=[0, 1, 2, 3])
            mk_main(wgb_rhs, wgb_wts, [0, 1, 2, 3], 0, close=True,
                    ks=[4, 5, 6, 7], opens=False)
            mk_main(wgb_rhs, wgb_wts, [4, 5, 6, 7], CH, ks=[0, 1, 2, 3])
            mk_main(wgb_rhs, wgb_wts, [4, 5, 6, 7], CH, close=True,
                    ks=[4, 5, 6, 7], opens=False)

            def f_wgbev(eng, zG=zG):
                for mt in range(8):
                    reg = 0 if mt < 4 else CH
                    eng.tensor_tensor(sl(zG, mt), sl(zG, mt),
                                      pmain[mt % 4][:, reg:reg + CH], OP.add)
                eng.drain()
            P.step("vector", f_wgbev,
                   reads=["pm0", "pm1", "pm2", "pm3", zn], writes=[zn])

            if nx1:
                x_rhs = [("spBC", sp_hi(spBC, k), sp_lo(spBC, k)) for k in range(8)]
                emit_stats("spBC", spBC)
                mk_main(x_rhs, wffn_wts, [0, 1, 2, 3], 0)

            b2h = b2.ap().bitcast(BF16)

            def f_zsh(eng, zG=zG):
                eng.activation(b2h[:, 0:2048], zG[:], AF.Copy)
            P.step("scalar", f_zsh, reads=[zn], writes=["b2"])

            def f_zsl(eng, zG=zG):
                eng.tensor_tensor(b2h[:, 2048:4096], zG[:], b2h[:, 0:2048], OP.subtract)
                eng.drain()
            P.step("vector", f_zsl, reads=[zn, "b2"], writes=["b2"])

            def f_zsq(eng, zG=zG):
                eng.activation(sq_sb[:], zG[:], AF.Square)
            P.step("scalar", f_zsq, reads=[zn], writes=["sq"])

            if nx1:
                emit_rows("mac", True)

            def f_zst(eng):
                ins = None
                for half in range(2):
                    for kt in range(8):
                        eng.matmul(pstat[:, 0:CH], ones_bf[:],
                                   b2h[:, 2048 * half + CH * kt:2048 * half + CH * (kt + 1)],
                                   start=(half == 0 and kt == 0), stop=(half == 1 and kt == 7))
                for kt in range(8):
                    ins = eng.matmul(pstat[:, CH:2 * CH], ones_r[:],
                                     sq_sb[:, CH * kt:CH * (kt + 1)],
                                     start=(kt == 0), stop=(kt == 7))
                return ins
            P.step("tensor", f_zst, reads=["b2", "sq", "ones"], writes=["pstat"])

            if nx1:
                emit_bc(False, a2_sb, "a2_sb")
                mk_corr(W2F, [0, 1, 2, 3], 0)
                mk_main(x_rhs, wffn_wts, [4, 5, 6, 7], CH)
                mk_corr(W2F, [4, 5, 6, 7], CH)

            emit_rows("att", False, need_corr=False)
            emit_bc(True, mu_lng=True)

            def f_znm(eng, zG=zG):
                eng.tensor_tensor(v8(zG), v8(zG), bcast8(pmb[:, 0:CH]), OP.add)
                eng.drain()
                eng.tensor_tensor(v8(zG), v8(zG), bcast8(prb[:, 0:CH]), OP.mult)
                eng.drain()
            P.step("vector", f_znm, reads=[zn, "prb", "pmb"], writes=[zn])

            def f_sg(eng, zG=zG):
                for kt in range(8):
                    eng.activation(sl(zG, kt), sl(zG, kt), AF.Sigmoid,
                                   bias=ccol("lngb", kt), scale=ccol("lngw", kt))
            P.step("scalar", f_sg, reads=[zn, "cols"], writes=[zn])

            def f_gate(eng, resA=resA, zG=zG):
                eng.tensor_tensor(b5[:], zG[:], b5[:], OP.mult)
                eng.drain()
                eng.tensor_tensor(resA[:], resA[:], b5[:], OP.add)
                eng.drain()
            P.step("vector", f_gate, reads=["b5", zn, rn],
                   writes=[rn, "b5"])

            if nx1:
                mk_evac(b2, "b2", [0, 1, 2, 3], 0, a2_sb, "a2_sb")
                mk_evac(b2, "b2", [4, 5, 6, 7], CH, a2_sb, "a2_sb")

                def f_ad1n(eng, zG=zG):
                    eng.tensor_tensor(zG[:], b3[:], b2[:], OP.add)
                    eng.drain()
                P.step("vector", f_ad1n, reads=["b3", "b2", zn], writes=[zn])

            emit_split(resA, rn, spBC, "spBC")
            f2_rhs = [("spBC", sp_hi(spBC, k), sp_lo(spBC, k)) for k in range(8)]
            emit_stats("spBC", spBC)
            emit_rows("mac", True)
            mk_main(f2_rhs, wffn_wts, [0, 1, 2, 3], 0)
            emit_bc(False)
            if nx1:
                emit_split(zG, zn, spA, "spA")
            mk_corr(W2F, [0, 1, 2, 3], 0)
            mk_main(f2_rhs, wffn_wts, [4, 5, 6, 7], CH)
            mk_corr(W2F, [4, 5, 6, 7], CH)
            mk_evac(b2, "b2", [0, 1, 2, 3], 0)
            mk_evac(b2, "b2", [4, 5, 6, 7], CH)

            def f_y3(eng, resA=resA):
                eng.tensor_tensor(resA[:], resA[:], b2[:], OP.add)
                eng.drain()
            P.step("vector", f_y3, reads=["b2", rn], writes=[rn])

            def f_ys(eng, c=c, resA=resA):
                eng.dma_start(out=yv[c],
                              in_=resA.ap().rearrange("p (kt s) -> p kt s", s=CH)
                              ).then_inc(q_y, 16)
            last_ystore = P.step("gpsimd", f_ys, reads=[rn], dma_sem="q_y")

        P.step("gpsimd", lambda eng: None, extra=[last_ystore])

        P.finalize()

        @block.sync
        def _(sync):
            P.emit("sync", sync, sems, qsems)

        @block.gpsimd
        def _(gp):
            P.emit("gpsimd", gp, sems, qsems)

        @block.tensor
        def _(tensor):
            P.emit("tensor", tensor, sems, qsems)

        @block.vector
        def _(vector):
            P.emit("vector", vector, sems, qsems)

        @block.scalar
        def _(scalar):
            P.emit("scalar", scalar, sems, qsems)

    return nc


def host_constants():
    Pm = _sigma()
    pis = _pi_consts()
    t32 = np.linspace(2.0, 8.0, 1024, dtype=np.float32)
    dec32 = (np.float32(0.5) ** (np.float32(1.0) / t32)).astype(np.float32)
    beta32 = (np.float32(1.0) - dec32).astype(np.float32)
    return Pm, pis, dec32, beta32


def kernel(L0_attn_x, L0_attn_Wt, L0_attn_Wo, L0_attn_Wg, L0_attn_bg,
           L0_attn_ln1_w, L0_attn_ln1_b, L0_attn_lnc_w, L0_attn_lnc_b,
           L0_attn_lng_w, L0_attn_lng_b, L0_ffn_G1_R, L0_ffn_G2_R,
           L0_ffn_G1_I, L0_ffn_G2_I, L0_ffn_bias_real, L0_ffn_bias_imag):
    Pm, pis, dec32, beta32 = host_constants()
    f32 = np.float32

    x = np.asarray(L0_attn_x, dtype=f32)
    WtP = np.ascontiguousarray(np.asarray(L0_attn_Wt, dtype=f32)[Pm, :])
    WoP = np.ascontiguousarray(np.asarray(L0_attn_Wo, dtype=f32)[:, Pm])
    Pg = np.concatenate([Pm, 1024 + Pm])
    WgP = np.ascontiguousarray(np.asarray(L0_attn_Wg, dtype=f32)[Pg][:, Pm])

    biasf = np.concatenate([np.asarray(L0_ffn_bias_real, dtype=f32),
                            np.asarray(L0_ffn_bias_imag, dtype=f32)])[Pm]
    bgP = np.asarray(L0_attn_bg, dtype=f32)[Pm]
    rowsc = np.ascontiguousarray(np.stack([biasf, beta32, bgP]).astype(f32))
    cols = np.zeros((128, _COL_W), dtype=f32)

    def put(name, vec):
        cols[:, _COL_OFF[name]:_COL_OFF[name] + 8] = _col_pack(vec)

    put("ln1w", np.asarray(L0_attn_ln1_w, dtype=f32)[Pm])
    put("ln1b", np.asarray(L0_attn_ln1_b, dtype=f32)[Pm])
    put("lngw", np.asarray(L0_attn_lng_w, dtype=f32)[Pm])
    put("lngb", np.asarray(L0_attn_lng_b, dtype=f32)[Pm])
    put("bg", np.asarray(L0_attn_bg, dtype=f32)[Pm])
    put("lncw", np.asarray(L0_attn_lnc_w, dtype=f32))
    put("lncb", np.asarray(L0_attn_lnc_b, dtype=f32))
    put("dec", dec32)

    G1R = np.asarray(L0_ffn_G1_R, dtype=f32)[0]
    G2R = np.asarray(L0_ffn_G2_R, dtype=f32)[:, :, 0, :]
    G1I = np.asarray(L0_ffn_G1_I, dtype=f32)[0]
    G2I = np.asarray(L0_ffn_G2_I, dtype=f32)[:, :, 0, :]
    g2l_r = np.ascontiguousarray(G2R.transpose(0, 2, 1).reshape(4, 256))
    g2l_i = np.ascontiguousarray(G2I.transpose(0, 2, 1).reshape(4, 256))
    g1r_r = np.ascontiguousarray(G1R.transpose(1, 0, 2).reshape(4, 1024))
    g1r_i = np.ascontiguousarray(G1I.transpose(1, 0, 2).reshape(4, 1024))

    nc = build_program()

    in_maps = []
    for i in range(NCORE):
        xT = np.ascontiguousarray(x[i][:, Pm].T)
        in_maps.append({
            "x": xT, "wt": WtP, "wo": WoP, "wg": WgP,
            "cols": cols, "rowsc": rowsc,
            "pis": pis.reshape(8 * 128, 128),
            "g2l_r": g2l_r, "g2l_i": g2l_i, "g1r_r": g1r_r, "g1r_i": g1r_i,
        })

    res = run_bass_kernel_spmd(nc, in_maps, core_ids=list(range(NCORE)))

    out = np.empty((NCORE, SEQ, D), dtype=f32)
    for i in range(NCORE):
        yT = np.asarray(res.results[i]["y"])
        out[i][:, Pm] = yT.T
    return out


# version 25
# speedup vs baseline: 1.0171x; 1.0007x over previous
import numpy as np
import concourse.bass as bass
import concourse.mybir as mybir
from concourse.bass_utils import run_bass_kernel_spmd

F32 = mybir.dt.float32
F32R = mybir.dt.float32r
BF16 = mybir.dt.bfloat16
AF = mybir.ActivationFunctionType
OP = mybir.AluOpType

D = 1024
SEQ = 2048
NCORE = 8
CH = 256
NCH = SEQ // CH
EPS_MAC = 1e-5
EPS_ATT = 1e-6
INV_D = 1.0 / 1024.0

_COL_NAMES = ["ln1w", "ln1b", "lngw", "lngb", "bg", "lncw", "lncb", "dec"]
_COL_OFF = {n: 8 * i for i, n in enumerate(_COL_NAMES)}
_COL_W = 8 * len(_COL_NAMES)


def _sigma():
    P = np.zeros(D, dtype=np.int64)
    for d_ in range(32):
        for g_ in range(16):
            P[32 * g_ + d_] = 16 * d_ + g_
            P[512 + 32 * g_ + d_] = 512 + 16 * d_ + g_
    return P


def _pi_consts():
    pis = np.zeros((8, 128, 128), dtype=np.float32)
    for neg in range(2):
        for par in range(2):
            for km in range(2):
                j = neg * 4 + par * 2 + km
                for p in range(128):
                    if (p // 32) % 2 != par:
                        continue
                    k = 64 * km + 32 * (p // 64) + (p % 32)
                    pis[j, k, p] = -1.0 if neg else 1.0
    return pis


def _col_pack(vec):
    return np.ascontiguousarray(vec.reshape(8, 128).T.astype(np.float32))


class _Step:
    __slots__ = ("engine", "fn", "deps", "idx", "is_dma", "sem_name", "dma_cum", "order")


class Prog:
    def __init__(self):
        self.steps = []
        self.writer = {}
        self.readers = {}
        self.dma_counts = {}

    def step(self, engine, fn, reads=(), writes=(), extra=(), dma_sem=None, ndma=1):
        st = _Step()
        st.engine = engine
        st.fn = fn
        st.deps = set()
        st.order = len(self.steps)
        for b in reads:
            w = self.writer.get(b)
            if w is not None:
                st.deps.add(w)
        for b in writes:
            w = self.writer.get(b)
            if w is not None:
                st.deps.add(w)
            for r in self.readers.get(b, ()):
                st.deps.add(r)
        for e in extra:
            if e is not None:
                st.deps.add(e)
        st.deps.discard(st)
        for b in reads:
            self.readers.setdefault(b, []).append(st)
        for b in writes:
            self.writer[b] = st
            self.readers[b] = [st]
        st.is_dma = dma_sem is not None
        st.sem_name = dma_sem
        if st.is_dma:
            self.dma_counts[dma_sem] = self.dma_counts.get(dma_sem, 0) + ndma
            st.dma_cum = self.dma_counts[dma_sem]
        for d in st.deps:
            assert d.order < st.order, "dependency on later step"
        self.steps.append(st)
        return st

    def finalize(self):
        idx = {}
        for st in self.steps:
            if not st.is_dma:
                idx[st.engine] = idx.get(st.engine, 0) + 1
                st.idx = idx[st.engine]

    def emit(self, engine, eng, sems, qsems):
        waited = {}
        for st in self.steps:
            if st.engine != engine:
                continue
            for d in sorted(st.deps, key=lambda s: s.order):
                if d.is_dma:
                    key = ("q", d.sem_name)
                    val = 16 * d.dma_cum
                    sem = qsems[d.sem_name]
                else:
                    if d.engine == engine:
                        continue
                    key = ("e", d.engine)
                    val = d.idx
                    sem = sems[d.engine]
                if waited.get(key, 0) < val:
                    eng.wait_ge(sem, val)
                    waited[key] = val
            ret = st.fn(eng)
            if not st.is_dma:
                if engine == "tensor" and ret is not None:
                    ret.then_inc(sems[engine], 1)
                elif engine == "tensor":
                    eng.nop().then_inc(sems[engine], 1)
                else:
                    eng.drain()
                    eng.sem_inc(sems[engine], 1)


def build_program():
    nc = bass.Bass("TRN2", target_bir_lowering=False)

    x_hbm = nc.dram_tensor("x", [D, SEQ], F32, kind="ExternalInput").ap()
    y_hbm = nc.dram_tensor("y", [D, SEQ], F32, kind="ExternalOutput").ap()
    wt_hbm = nc.dram_tensor("wt", [D, D], F32, kind="ExternalInput").ap()
    wo_hbm = nc.dram_tensor("wo", [D, D], F32, kind="ExternalInput").ap()
    wg_hbm = nc.dram_tensor("wg", [2 * D, D], F32, kind="ExternalInput").ap()
    cols_hbm = nc.dram_tensor("cols", [128, _COL_W], F32, kind="ExternalInput").ap()
    rowsc_hbm = nc.dram_tensor("rowsc", [3, D], F32, kind="ExternalInput").ap()
    pis_hbm = nc.dram_tensor("pis", [8 * 128, 128], F32, kind="ExternalInput").ap()
    g2l_r_h = nc.dram_tensor("g2l_r", [4, 256], F32, kind="ExternalInput").ap()
    g2l_i_h = nc.dram_tensor("g2l_i", [4, 256], F32, kind="ExternalInput").ap()
    g1r_r_h = nc.dram_tensor("g1r_r", [4, 1024], F32, kind="ExternalInput").ap()
    g1r_i_h = nc.dram_tensor("g1r_i", [4, 1024], F32, kind="ExternalInput").ap()
    wgspill = nc.dram_tensor("wgspill", [16, 128, 2048], BF16, kind="Internal").ap()

    import contextlib
    es = contextlib.ExitStack()
    with es:
        def sb(name, shape, dt=F32):
            return es.enter_context(nc.sbuf_tensor(name, shape, dt))

        def psum(name, shape):
            return es.enter_context(nc.psum_tensor(name, shape, F32))

        b1 = sb("b1", [128, 2048])
        b2 = sb("b2", [128, 2048])
        b3 = sb("b3", [128, 2048])
        b5 = sb("b5", [128, 2048])
        b6 = sb("b6", [128, 2048])

        wffn_h = sb("wffn_h", [128, 8192], BF16)
        wffn_l = sb("wffn_l", [128, 8192], BF16)
        wt_h = sb("wt_h", [128, 8192], BF16)
        wt_l = sb("wt_l", [128, 8192], BF16)
        wo_h = sb("wo_h", [128, 8192], BF16)
        wo_l = sb("wo_l", [128, 8192], BF16)
        wgs = [sb(f"wgs{i}", [128, 8192], BF16) for i in range(2)]
        spA = sb("spA", [128, 4096], BF16)
        spBC = sb("spBC", [128, 4096], BF16)

        sq_sb = sb("sq_sb", [128, 2048], F32R)
        cols = sb("cols_sb", [128, _COL_W])
        carry = sb("carry", [128, 8])
        ones_bf = sb("ones_bf", [128, 128], BF16)
        ones_r = sb("ones_r", [128, 128], F32R)
        ones_row = sb("ones_row", [1, 128])
        ones_col = sb("ones_col", [128, 1])
        a_sb = sb("a_sb", [128, CH])
        a2_sb = sb("a2_sb", [128, CH])
        r2 = sb("r2", [66, CH])
        r_t = sb("r_t", [66, CH])
        r_a = sb("r_a", [66, CH])
        r_rho = sb("r_rho", [2, CH])
        r_lm = a_sb
        w2all = sb("w2all", [66, 2048])
        W2F, W2T, W2O = 0, 32, 64
        W2G = (0, 1024)
        g2l_r = wffn_h[0:4, 0:512].bitcast(F32)
        g2l_i = wffn_l[0:4, 0:512].bitcast(F32)
        g1r_r = wffn_h[0:4, 512:2560].bitcast(F32)
        g1r_i = wffn_l[0:4, 512:2560].bitcast(F32)

        s_sb = wgs[0][:, 0:4096].bitcast(F32)
        u_r = wgs[0][:, 4096:8192].bitcast(F32)
        u_i = wgs[1][:, 0:4096].bitcast(F32)
        pis_sb = wgs[1][:, 4096:6144].bitcast(F32)

        pmain = [psum(f"pm{i}", [128, 512]) for i in range(4)]
        pstat = psum("pstat", [128, 512])
        prb = psum("prb", [128, 512])
        pmb = psum("pmb", [128, 512])
        pdec = psum("pdec", [128, 512])

        sem_T = es.enter_context(nc.semaphore("s_T"))
        sem_V = es.enter_context(nc.semaphore("s_V"))
        sem_S = es.enter_context(nc.semaphore("s_S"))
        sem_G = es.enter_context(nc.semaphore("s_G"))
        q_c = es.enter_context(nc.semaphore("q_c"))
        q_wA = es.enter_context(nc.semaphore("q_wA"))
        q_wB = es.enter_context(nc.semaphore("q_wB"))
        q_x = es.enter_context(nc.semaphore("q_x"))
        q_y = es.enter_context(nc.semaphore("q_y"))
        q_s = [es.enter_context(nc.semaphore(f"q_s{i}")) for i in range(4)]
        q_r = es.enter_context(nc.semaphore("q_r"))
        block = es.enter_context(nc.Block())

        sems = {"tensor": sem_T, "vector": sem_V, "scalar": sem_S, "gpsimd": sem_G}
        qsems = {"q_c": q_c, "q_wA": q_wA, "q_wB": q_wB, "q_x": q_x, "q_y": q_y, "q_r": q_r,
                 "q_s0": q_s[0], "q_s1": q_s[1], "q_s2": q_s[2], "q_s3": q_s[3]}

        P = Prog()

        def sl(slab, kt, w=CH):
            return slab[:, w * kt:w * (kt + 1)]

        def ccol(name, kt):
            o = _COL_OFF[name] + kt
            return cols[:, o:o + 1]

        def bcast8(prow):
            return prow.unsqueeze(1).broadcast_to((128, 8, CH))

        def v8(slab):
            return slab.ap().rearrange("p (c s) -> p c s", s=CH)

        def f_consts(eng):
            eng.dma_start(out=cols[:], in_=cols_hbm).then_inc(q_c, 16)
            eng.dma_start(out=pis_sb.rearrange("p (j k) -> p j k", j=8),
                          in_=pis_hbm.rearrange("(j p) k -> p j k", p=128)).then_inc(q_c, 16)
            eng.dma_start(out=g2l_r[:], in_=g2l_r_h).then_inc(q_c, 16)
            eng.dma_start(out=g2l_i[:], in_=g2l_i_h).then_inc(q_c, 16)
            eng.dma_start(out=g1r_r[:], in_=g1r_r_h).then_inc(q_c, 16)
            eng.dma_start(out=g1r_i[:], in_=g1r_i_h).then_inc(q_c, 16)
            eng.dma_start(out=w2all[W2F + 1:W2F + 2, 0:1024], in_=rowsc_hbm[0:1, :]).then_inc(q_c, 16)
            eng.dma_start(out=b2[0:1, 1024:2048], in_=rowsc_hbm[1:2, :]).then_inc(q_c, 16)
        P.step("sync", f_consts, writes=["cols", "wgs1", "wffn_h", "wffn_l", "w2all", "b2"],
               dma_sem="q_c", ndma=8)

        def f_init(eng):
            eng.memset(ones_bf[:], 1.0)
            eng.memset(ones_row[:], 1.0)
            eng.memset(ones_col[:], 1.0)
            eng.memset(carry[:], 0.0)
            eng.memset(pdec[:], 1.0)
            eng.drain()
            eng.tensor_copy(ones_r[:].bitcast(F32R), ones_bf[:])
            eng.drain()
        P.step("vector", f_init, writes=["ones", "carry", "pdec"])

        def f_dec(eng):
            for g in range(8):
                eng.tensor_scalar(pdec[:, 64 * g:64 * (g + 1)],
                                  pdec[:, 64 * g:64 * (g + 1)],
                                  ccol("dec", g), None, OP.mult)
            eng.drain()
        P.step("vector", f_dec, reads=["cols", "pdec"], writes=["pdec"])

        for ri, (g2l_t, g1r_t, u_t, uname) in enumerate(
                [(g2l_r, g1r_r, u_r, "wgs0"), (g2l_i, g1r_i, u_i, "wgs1")]):
            for t in range(2):
                for n in range(2):
                    pb = pmain[(t * 2 + n) % 2]
                    pname = f"pm{(t * 2 + n) % 2}"

                    def f_smm(eng, g2l_t=g2l_t, g1r_t=g1r_t, t=t, n=n, pb=pb):
                        return eng.matmul(pb[:, 0:512], g2l_t[:, 128 * t:128 * (t + 1)],
                                          g1r_t[:, 512 * n:512 * (n + 1)],
                                          start=True, stop=True)
                    P.step("tensor", f_smm, reads=["wffn_h", "wffn_l"], writes=[pname])

                    def f_sev(eng, t=t, n=n, pb=pb):
                        eng.copy(s_sb[:, 1024 * t + 512 * n:1024 * t + 512 * n + 512],
                                 pb[:, 0:512])
                    P.step("scalar", f_sev, reads=[pname], writes=["wgs0"])

            def f_tr(eng, u_t=u_t):
                eng.transpose(u_t[:, 0:1024], s_sb[:, 0:1024])
                eng.transpose(u_t[:, 1024:2048], s_sb[:, 1024:2048])
                eng.drain()
            P.step("vector", f_tr, reads=["wgs0"], writes=[uname])

        for kt in range(8):
            ktl = kt % 4
            tu = ktl // 2
            km = ktl % 2
            for half in range(2):
                if kt < 4:
                    u_t, uname, neg = (u_r, "wgs0", 0) if half == 0 else (u_i, "wgs1", 0)
                else:
                    u_t, uname, neg = (u_i, "wgs1", 1) if half == 0 else (u_r, "wgs0", 0)
                bi = 2 + (2 * kt + half) % 2
                pb, pbn = pmain[bi], f"pm{bi}"

                def f_pmm(eng, u_t=u_t, tu=tu, km=km, neg=neg, pb=pb):
                    uview = u_t[:, 1024 * tu:1024 * (tu + 1)].rearrange(
                        "p (b two e) -> p two e b", b=32, two=2, e=16)
                    ins = None
                    for par in range(2):
                        j = neg * 4 + par * 2 + km
                        ins = eng.matmul(pb[:, 0:512], pis_sb[:, 128 * j:128 * (j + 1)],
                                         uview[:, par, :, :], start=(par == 0), stop=(par == 1))
                    return ins
                P.step("tensor", f_pmm, reads=[uname, "wgs1"], writes=[pbn])

                def f_pevh(eng, kt=kt, half=half, pb=pb):
                    o = 1024 * kt + 512 * half
                    eng.activation(wffn_h[:, o:o + 512], pb[:, 0:512], AF.Copy)
                P.step("scalar", f_pevh, reads=[pbn], writes=["wffn_h"])

                def f_pevl(eng, kt=kt, half=half, pb=pb):
                    o = 1024 * kt + 512 * half
                    eng.tensor_tensor(wffn_l[:, o:o + 512], pb[:, 0:512],
                                      wffn_h[:, o:o + 512], OP.subtract)
                    eng.drain()
                P.step("vector", f_pevl, reads=[pbn, "wffn_h"], writes=["wffn_l"])

        def f_w1f(eng):
            ins = None
            for half in range(2):
                pb = pmain[half]
                for kt in range(8):
                    eng.matmul(pb[0:1, 0:512], ones_bf[:, 0:1],
                               wffn_h[:, 1024 * kt + 512 * half:1024 * kt + 512 * (half + 1)],
                               start=(kt == 0), stop=False)
                    ins = eng.matmul(pb[0:1, 0:512], ones_bf[:, 0:1],
                                     wffn_l[:, 1024 * kt + 512 * half:1024 * kt + 512 * (half + 1)],
                                     start=False, stop=(kt == 7))
            return ins
        P.step("tensor", f_w1f, reads=["wffn_h", "wffn_l", "ones"],
               writes=["pm0", "pm1"])

        def f_w1fev(eng):
            eng.activation(w2all[W2F:W2F + 1, 0:512], pmain[0][0:1, 0:512], AF.Copy)
            eng.activation(w2all[W2F:W2F + 1, 512:1024], pmain[1][0:1, 0:512], AF.Copy)
        P.step("scalar", f_w1fev, reads=["pm0", "pm1"], writes=["w2all"])

        wt_v = wt_hbm.rearrange("(kt p) m -> kt p m", p=128)
        wo_v = wo_hbm.rearrange("(kt p) m -> kt p m", p=128)
        wg_v = wg_hbm.rearrange("(kt p) m -> kt p m", p=128)

        stage_bufs = [(b3, "b3"), (b5, "b5"), (b6, "b6"), (b1, "b1")]

        def f_bbc(eng):
            eng.matmul(pmain[2][:, 0:512], ones_row[0:1, :], b2[0:1, 1024:1536],
                       start=True, stop=True)
            return eng.matmul(pmain[3][:, 0:512], ones_row[0:1, :], b2[0:1, 1536:2048],
                              start=True, stop=True)
        P.step("tensor", f_bbc, reads=["b2", "ones"], writes=["pm2", "pm3"])

        def emit_wsplit(tag, src_v, kts, dsth, dsthname, dstl, dstlname,
                        scale_col, bias_col, beta_scale, rot0=0):
            for i0 in range(0, len(kts), 2):
                pair = kts[i0:i0 + 2]
                ri = (rot0 + i0 // 2) % 4
                stg, sn = stage_bufs[ri]
                qs = f"q_s{ri}"

                def f_ld(eng, pair=pair, stg=stg, qs=qs):
                    for j, kt in enumerate(pair):
                        eng.dma_start(out=stg[:, 1024 * j:1024 * (j + 1)],
                                      in_=src_v[kt]).then_inc(qsems[qs], 16)
                P.step("sync", f_ld, writes=[sn], dma_sem=qs, ndma=len(pair))

                if beta_scale:
                    def f_bs(eng, pair=pair, stg=stg):
                        for j in range(len(pair)):
                            eng.tensor_tensor(stg[:, 1024 * j:1024 * j + 512],
                                              stg[:, 1024 * j:1024 * j + 512],
                                              pmain[2][:, 0:512], OP.mult)
                            eng.tensor_tensor(stg[:, 1024 * j + 512:1024 * (j + 1)],
                                              stg[:, 1024 * j + 512:1024 * (j + 1)],
                                              pmain[3][:, 0:512], OP.mult)
                        eng.drain()
                    P.step("vector", f_bs, reads=[sn, "pm2", "pm3"], writes=[sn])

                def f_brow(eng, pair=pair, stg=stg):
                    ins = None
                    for j, kt in enumerate(pair):
                        eng.matmul(pstat[0:1, 0:512], ccol(bias_col, kt),
                                   stg[:, 1024 * j:1024 * j + 512],
                                   start=(kt == 0), stop=(kt == 7))
                        ins = eng.matmul(prb[0:1, 0:512], ccol(bias_col, kt),
                                         stg[:, 1024 * j + 512:1024 * (j + 1)],
                                         start=(kt == 0), stop=(kt == 7))
                    return ins
                P.step("tensor", f_brow, reads=[sn, "cols"], writes=["pstat", "prb"])

                def f_sc(eng, pair=pair, stg=stg):
                    for j, kt in enumerate(pair):
                        eng.tensor_scalar(stg[:, 1024 * j:1024 * (j + 1)],
                                          stg[:, 1024 * j:1024 * (j + 1)],
                                          ccol(scale_col, kt), None, OP.mult)
                    eng.drain()
                P.step("vector", f_sc, reads=[sn, "cols"], writes=[sn])

                def f_w1(eng, pair=pair, stg=stg):
                    ins = None
                    for j, kt in enumerate(pair):
                        eng.matmul(pmain[0][0:1, 0:512], ones_col[:],
                                   stg[:, 1024 * j:1024 * j + 512],
                                   start=(kt == 0), stop=(kt == 7))
                        ins = eng.matmul(pmain[1][0:1, 0:512], ones_col[:],
                                         stg[:, 1024 * j + 512:1024 * (j + 1)],
                                         start=(kt == 0), stop=(kt == 7))
                    return ins
                P.step("tensor", f_w1, reads=[sn, "ones"], writes=["pm0", "pm1"])

                def f_hi(eng, pair=pair, stg=stg):
                    for j, kt in enumerate(pair):
                        eng.activation(sl(dsth, kt, 1024),
                                       stg[:, 1024 * j:1024 * (j + 1)], AF.Copy)
                P.step("scalar", f_hi, reads=[sn], writes=[dsthname])

                def f_lo(eng, pair=pair, stg=stg):
                    for j, kt in enumerate(pair):
                        eng.tensor_tensor(sl(dstl, kt, 1024),
                                          stg[:, 1024 * j:1024 * (j + 1)],
                                          sl(dsth, kt, 1024), OP.subtract)
                    eng.drain()
                P.step("vector", f_lo, reads=[sn, dsthname], writes=[dstlname])

        def emit_wrows(base):
            def f_ev(eng):
                eng.activation(w2all[base:base + 1, 0:512], pmain[0][0:1, 0:512], AF.Copy)
                eng.activation(w2all[base:base + 1, 512:1024], pmain[1][0:1, 0:512], AF.Copy)
                eng.activation(b2[0:1, 0:512], pstat[0:1, 0:512], AF.Copy)
                eng.activation(b2[0:1, 512:1024], prb[0:1, 0:512], AF.Copy)
            P.step("scalar", f_ev, reads=["pm0", "pm1", "pstat", "prb"],
                   writes=["w2all", "b2"])

            def f_dma(eng):
                eng.dma_start(out=w2all[base + 1:base + 2, 0:1024],
                              in_=b2[0:1, 0:1024]).then_inc(q_c, 16)
            P.step("sync", f_dma, reads=["b2"], writes=["w2all"], dma_sem="q_c")

        emit_wsplit("wt", wt_v, list(range(8)), wt_h, "wt_h", wt_l, "wt_l",
                    scale_col="ln1w", bias_col="ln1b", beta_scale=True, rot0=0)
        emit_wrows(W2T)
        emit_wsplit("wo", wo_v, list(range(8)), wo_h, "wo_h", wo_l, "wo_l",
                    scale_col="lncw", bias_col="lncb", beta_scale=False, rot0=0)
        emit_wrows(W2O)

        def f_bgdma(eng):
            eng.dma_start(out=b2[0:1, 1024:2048],
                          in_=rowsc_hbm[2:3, :]).then_inc(q_c, 16)
        P.step("sync", f_bgdma, writes=["b2"], dma_sem="q_c")

        wg_stage = [(b3, "b3"), (b5, "b5")]
        wg_out = [(spBC, "spBC"), (spA, "spA")]

        def emit_wg_pair(i0):
            pair = [i0, i0 + 1]
            ri = (i0 // 2) % 2
            stg, sn = wg_stage[ri]
            out, on = wg_out[ri]
            qs = f"q_s{ri}"

            def f_ld(eng, pair=pair, stg=stg, qs=qs):
                for j, kt in enumerate(pair):
                    eng.dma_start(out=stg[:, 1024 * j:1024 * (j + 1)],
                                  in_=wg_v[kt]).then_inc(qsems[qs], 16)
            P.step("sync", f_ld, writes=[sn], dma_sem=qs, ndma=2)

            if i0 < 8:
                def f_gbrow(eng, pair=pair, stg=stg):
                    ins = None
                    for j, kt in enumerate(pair):
                        eng.matmul(pstat[0:1, 0:512], ccol("ln1b", kt),
                                   stg[:, 1024 * j:1024 * j + 512],
                                   start=(kt == 0), stop=(kt == 7))
                        ins = eng.matmul(prb[0:1, 0:512], ccol("ln1b", kt),
                                         stg[:, 1024 * j + 512:1024 * (j + 1)],
                                         start=(kt == 0), stop=(kt == 7))
                    return ins
                P.step("tensor", f_gbrow, reads=[sn, "cols"], writes=["pstat", "prb"])

                def f_gsc(eng, pair=pair, stg=stg):
                    for j, kt in enumerate(pair):
                        eng.tensor_scalar(stg[:, 1024 * j:1024 * (j + 1)],
                                          stg[:, 1024 * j:1024 * (j + 1)],
                                          ccol("ln1w", kt), None, OP.mult)
                    eng.drain()
                P.step("vector", f_gsc, reads=[sn, "cols"], writes=[sn])

                def f_gw1(eng, pair=pair, stg=stg):
                    ins = None
                    for j, kt in enumerate(pair):
                        eng.matmul(pmain[0][0:1, 0:512], ones_col[:],
                                   stg[:, 1024 * j:1024 * j + 512],
                                   start=(kt == 0), stop=(kt == 7))
                        ins = eng.matmul(pmain[1][0:1, 0:512], ones_col[:],
                                         stg[:, 1024 * j + 512:1024 * (j + 1)],
                                         start=(kt == 0), stop=(kt == 7))
                    return ins
                P.step("tensor", f_gw1, reads=[sn, "ones"], writes=["pm0", "pm1"])

            def f_hi(eng, pair=pair, stg=stg, out=out):
                for j, kt in enumerate(pair):
                    eng.activation(out[:, 2048 * j:2048 * j + 1024],
                                   stg[:, 1024 * j:1024 * (j + 1)], AF.Copy)
            P.step("scalar", f_hi, reads=[sn], writes=[on])

            def f_lo(eng, pair=pair, stg=stg, out=out):
                for j, kt in enumerate(pair):
                    eng.tensor_tensor(out[:, 2048 * j + 1024:2048 * (j + 1)],
                                      stg[:, 1024 * j:1024 * (j + 1)],
                                      out[:, 2048 * j:2048 * j + 1024], OP.subtract)
                eng.drain()
            P.step("vector", f_lo, reads=[sn, on], writes=[on])

            def f_sp(eng, pair=pair, out=out):
                for j, kt in enumerate(pair):
                    eng.dma_start(out=wgspill[kt],
                                  in_=out[:, 2048 * j:2048 * (j + 1)]).then_inc(q_wB, 16)
            P.step("sync", f_sp, reads=[on], writes=["wgspill"],
                   dma_sem="q_wB", ndma=2)

        for i0 in range(0, 8, 2):
            emit_wg_pair(i0)

        def f_gwev(eng):
            eng.activation(w2all[0:1, 1024:1536], pmain[0][0:1, 0:512], AF.Copy)
            eng.activation(w2all[0:1, 1536:2048], pmain[1][0:1, 0:512], AF.Copy)
            eng.activation(b2[0:1, 0:512], pstat[0:1, 0:512], AF.Copy)
            eng.activation(b2[0:1, 512:1024], prb[0:1, 0:512], AF.Copy)
        P.step("scalar", f_gwev, reads=["pm0", "pm1", "pstat", "prb"],
               writes=["w2all", "b2"])

        def f_bgadd(eng):
            eng.tensor_tensor(b2[0:1, 0:1024], b2[0:1, 0:1024],
                              b2[0:1, 1024:2048], OP.add)
            eng.drain()
        P.step("vector", f_bgadd, reads=["b2"], writes=["b2"])

        def f_gwdma(eng):
            eng.dma_start(out=w2all[1:2, 1024:2048],
                          in_=b2[0:1, 0:1024]).then_inc(q_c, 16)
        P.step("sync", f_gwdma, reads=["b2"], writes=["w2all"], dma_sem="q_c")

        def sp_hi(sp, kt):
            return sp[:, CH * kt:CH * (kt + 1)]

        def sp_lo(sp, kt):
            return sp[:, 2048 + CH * kt:2048 + CH * (kt + 1)]

        def wtile(buf, kt):
            return buf[:, 1024 * kt:1024 * (kt + 1)]

        def wg_w(kt):
            buf = wgs[(kt // 4) % 2]
            lk = kt % 4
            return (f"wgs{(kt // 4) % 2}",
                    buf[:, 2048 * lk:2048 * lk + 1024],
                    buf[:, 2048 * lk + 1024:2048 * (lk + 1)])

        def emit_split(src, srcname, sp, spname, with_sq=True):
            def f_hi(eng):
                eng.activation(sp[:, 0:2048], src[:], AF.Copy)
            P.step("scalar", f_hi, reads=[srcname], writes=[spname])

            def f_lo(eng):
                eng.tensor_tensor(sp[:, 2048:4096], src[:], sp[:, 0:2048], OP.subtract)
                eng.drain()
            P.step("vector", f_lo, reads=[srcname, spname], writes=[spname])

            if with_sq:
                def f_sq(eng):
                    eng.activation(sq_sb[:], src[:], AF.Square)
                P.step("scalar", f_sq, reads=[srcname], writes=["sq"])

        def emit_stats(spname, sp):
            def f_st(eng):
                ins = None
                for half in range(2):
                    for kt in range(8):
                        eng.matmul(pstat[:, 0:CH], ones_bf[:],
                                   sp[:, 2048 * half + CH * kt:2048 * half + CH * (kt + 1)],
                                   start=(half == 0 and kt == 0), stop=(half == 1 and kt == 7))
                for kt in range(8):
                    ins = eng.matmul(pstat[:, CH:2 * CH], ones_r[:],
                                     sq_sb[:, CH * kt:CH * (kt + 1)],
                                     start=(kt == 0), stop=(kt == 7))
                return ins
            P.step("tensor", f_st, reads=[spname, "sq", "ones"], writes=["pstat"])

        def emit_rows(kind, half, need_corr=True):
            nl = 66 if need_corr else 1
            mu = r2 if need_corr else r_lm

            def f_row1(eng):
                eng.tensor_scalar(mu[0:nl, :], pstat[0:nl, 0:CH], -INV_D, None, OP.mult)
                eng.drain()
                eng.tensor_tensor(r_t[0:nl, :], mu[0:nl, :], mu[0:nl, :], OP.mult)
                eng.drain()
                eng.scalar_tensor_tensor(r_a[0:nl, :], pstat[0:nl, CH:2 * CH], INV_D,
                                         r_t[0:nl, :], OP.mult, OP.subtract)
                eng.drain()
                if kind == "fused":
                    eng.tensor_scalar(r_a[0:nl, :], r_a[0:nl, :], 1.0 + EPS_ATT,
                                      EPS_MAC * EPS_ATT, OP.mult, OP.add)
                    eng.drain()
                else:
                    eps = EPS_MAC if kind == "mac" else EPS_ATT
                    eng.tensor_scalar(r_a[0:nl, :], r_a[0:nl, :], eps, None, OP.add)
                    eng.drain()
            P.step("vector", f_row1, reads=["pstat"],
                   writes=(["r2", "rows", "rows2"] if need_corr
                           else ["a_sb", "rows", "rows2"]))

            def f_sqrt(eng):
                eng.activation(r_t[0:nl, :], r_a[0:nl, :], AF.Sqrt)
                if not need_corr:
                    eng.drain()
                    eng.activation(r_a[0:1, 0:1], r_a[0:1, 0:1], AF.Sigmoid)
            P.step("scalar", f_sqrt, reads=["rows"],
                   writes=(["rows2"] if need_corr else ["rows2", "rows"]))

            def f_row2(eng):
                eng.reciprocal(r_rho[0:1, :], r_t[0:1, :])
                eng.drain()
                if half:
                    eng.tensor_scalar(r_rho[0:1, :], r_rho[0:1, :], 0.5, None, OP.mult)
                    eng.drain()
            P.step("vector", f_row2, reads=["rows2"], writes=["rows3"])

            if need_corr:
                def f_rdma(eng):
                    for i in (1, 33, 65):
                        eng.dma_start(out=r2[i:i + 1, :],
                                      in_=r_t[0:1, :]).then_inc(q_r, 16)
                P.step("sync", f_rdma, reads=["rows2", "r2"], writes=["r2"],
                       dma_sem="q_r", ndma=3)

        def emit_bc(negmu_bc, ab=None, abname=None, mu_lng=False):
            if ab is None:
                ab, abname = a_sb, "a_sb"
            mu, mun = (r_lm, "a_sb") if mu_lng else (r2, "r2")

            def f_bc(eng):
                ins = eng.matmul(prb[:, 0:CH], ones_row[0:1, :], r_rho[0:1, :],
                                 start=True, stop=True)
                if negmu_bc:
                    ins = eng.matmul(pmb[:, 0:CH], ones_row[0:1, :], mu[0:1, :],
                                     start=True, stop=True)
                return ins
            P.step("tensor", f_bc, reads=["rows3", mun, "ones"],
                   writes=["prb"] + (["pmb"] if negmu_bc else []))

            def f_acp(eng):
                eng.activation(ab[:], prb[:, 0:CH], AF.Copy)
            P.step("scalar", f_acp, reads=["prb"], writes=[abname])

        def mk_main(rhs, wts, mts, reg, close=False, ks=None, opens=True):
            if ks is None:
                ks = list(range(8))
            rnames = sorted({n for n, _, _ in (rhs[k] for k in ks)})
            wnames = sorted({n for n, _, _ in (wts[k] for k in ks)})

            def f_mm(eng):
                ins = None
                first = True
                for pas in range(3):
                    for i in ks:
                        _, xh, xl = rhs[i]
                        _, wh, wl = wts[i]
                        xap = xh if pas in (0, 1) else xl
                        wsel = wh if pas in (0, 2) else wl
                        for mt in mts:
                            ins = eng.matmul(
                                pmain[mt % 4][:, reg:reg + CH],
                                wsel[:, 128 * mt:128 * (mt + 1)], xap,
                                start=(opens and pas == 0 and first),
                                stop=(close and pas == 2 and i == ks[-1]))
                        first = False
                return ins
            P.step("tensor", f_mm, reads=rnames + wnames,
                   writes=[f"pm{m % 4}" for m in mts])

        def mk_corr(base, mts, reg, coff=0):
            def f_c(eng):
                ins = None
                for mt in mts:
                    ins = eng.matmul(pmain[mt % 4][:, reg:reg + CH],
                                     w2all[base:base + 2,
                                           coff + 128 * mt:coff + 128 * (mt + 1)],
                                     r2[base:base + 2, 0:CH], start=False, stop=True)
                return ins
            P.step("tensor", f_c, reads=["r2", "w2all"],
                   writes=[f"pm{m % 4}" for m in mts])

        def mk_evac(dst, dstname, mts, reg, ab=None, abname=None):
            if ab is None:
                ab, abname = a_sb, "a_sb"

            def f_e(eng):
                for mt in mts:
                    eng.tensor_tensor(sl(dst, mt), pmain[mt % 4][:, reg:reg + CH],
                                      ab[:], OP.mult)
                eng.drain()
            P.step("vector", f_e,
                   reads=[f"pm{m % 4}" for m in mts] + [abname], writes=[dstname])

        def emit_site_mms(sp, spname, wts, base, dst, dstname,
                          kind, half, negmu_bc=False, ab=None, abname=None):
            rhs = [(spname, sp_hi(sp, k), sp_lo(sp, k)) for k in range(8)]
            emit_stats(spname, sp)
            emit_rows(kind, half)
            mk_main(rhs, wts, [0, 1, 2, 3], 0)
            emit_bc(negmu_bc, ab, abname)
            mk_corr(base, [0, 1, 2, 3], 0)
            mk_main(rhs, wts, [4, 5, 6, 7], CH)
            mk_corr(base, [4, 5, 6, 7], CH)
            mk_evac(dst, dstname, [0, 1, 2, 3], 0, ab, abname)
            mk_evac(dst, dstname, [4, 5, 6, 7], CH, ab, abname)

        xv = x_hbm.rearrange("(kt p) (c s) -> c p kt s", p=128, s=CH)
        yv = y_hbm.rearrange("(kt p) (c s) -> c p kt s", p=128, s=CH)

        wffn_wts = [("wffn_h", wtile(wffn_h, k), wtile(wffn_l, k)) for k in range(8)]
        wt_wts = [("wt_h", wtile(wt_h, k), wtile(wt_l, k)) for k in range(8)]
        wo_wts = [("wo_h", wtile(wo_h, k), wtile(wo_l, k)) for k in range(8)]

        def emit_wgl(q):
            def f_wgl(eng, q=q):
                eng.dma_start(
                    out=wgs[q % 2].ap().rearrange("p (kt m) -> p kt m", m=2048),
                    in_=wgspill[4 * q:4 * (q + 1)].rearrange("kt p m -> p kt m")
                ).then_inc(q_wB if q % 2 else q_wA, 16)
            P.step("sync", f_wgl, reads=["wgspill"], writes=[f"wgs{q % 2}"],
                   dma_sem=("q_wB" if q % 2 else "q_wA"))

        last_ystore = None
        for c in range(NCH):
            resA, rn = (b1, "b1") if c % 2 == 0 else (b6, "b6")
            zG, zn = (b6, "b6") if c % 2 == 0 else (b1, "b1")

            emit_wgl(0)
            emit_wgl(1)

            if c == 0:
                def f_xl0(eng):
                    eng.dma_start(out=b3.ap().rearrange("p (kt s) -> p kt s", s=CH),
                                  in_=xv[0]).then_inc(q_x, 16)
                P.step("gpsimd", f_xl0, writes=["b3"], dma_sem="q_x")
                emit_split(b3, "b3", spA, "spA")
                emit_site_mms(spA, "spA", wffn_wts, W2F, b5, "b5", "mac", True,
                              ab=a2_sb, abname="a2_sb")

            if c == 0:
                def f_ad1(eng, resA=resA):
                    eng.tensor_tensor(resA[:], b3[:], b5[:], OP.add)
                    eng.drain()
                P.step("vector", f_ad1, reads=["b3", "b5"], writes=[rn])
                emit_wg_pair(8)
                emit_wg_pair(10)
                emit_split(resA, rn, spA, "spA")

            emit_site_mms(spA, "spA", wt_wts, W2T, b2, "b2",
                          "fused", False, negmu_bc=True, ab=a2_sb, abname="a2_sb")
            if c == 0:
                emit_wg_pair(12)

            def f_scan(eng):
                for sub in range(4):
                    for g in range(8):
                        o = CH * g + 64 * sub
                        init = carry[:, g:g + 1] if sub == 0 else b2[:, o - 1:o]
                        eng.tensor_tensor_scan(b2[:, o:o + 64],
                                               pdec[:, 64 * g:64 * (g + 1)],
                                               b2[:, o:o + 64],
                                               init, OP.mult, OP.add)
                    eng.drain()
                for g in range(8):
                    eng.tensor_copy(carry[:, g:g + 1], b2[:, CH * g + CH - 1:CH * g + CH])
                eng.drain()
            P.step("vector", f_scan, reads=["b2", "pdec", "carry"],
                   writes=["b2", "carry"])

            wga_rhs = [("spA", sp_hi(spA, k), sp_lo(spA, k)) for k in range(8)]
            wga_wts = [wg_w(k) for k in range(8)]
            mk_main(wga_rhs, wga_wts, [0, 1, 2, 3], 0, ks=[0, 1, 2, 3])
            mk_main(wga_rhs, wga_wts, [0, 1, 2, 3], 0, ks=[4, 5, 6, 7],
                    opens=False)
            mk_corr(W2G[0], [0, 1, 2, 3], 0, coff=W2G[1])
            mk_main(wga_rhs, wga_wts, [4, 5, 6, 7], CH, ks=[0, 1, 2, 3])
            mk_main(wga_rhs, wga_wts, [4, 5, 6, 7], CH, ks=[4, 5, 6, 7],
                    opens=False)
            mk_corr(W2G[0], [4, 5, 6, 7], CH, coff=W2G[1])
            emit_split(b2, "b2", spBC, "spBC")
            mk_evac(zG, zn, [0, 1, 2, 3], 0, a2_sb, "a2_sb")
            mk_evac(zG, zn, [4, 5, 6, 7], CH, a2_sb, "a2_sb")
            if c == 0:
                emit_wg_pair(14)

            emit_wgl(2)
            emit_wgl(3)

            def f_xm(eng, resA=resA):
                eng.tensor_tensor(v8(b3), v8(resA), bcast8(pmb[:, 0:CH]), OP.add)
                eng.drain()
                eng.tensor_tensor(v8(b3), v8(b3),
                                  a2_sb[:].unsqueeze(1).broadcast_to((128, 8, CH)),
                                  OP.mult)
                eng.drain()
            P.step("vector", f_xm, reads=[rn, "a2_sb", "pmb"], writes=["b3"])

            def f_xaf(eng):
                for kt in range(8):
                    eng.activation(sl(b3, kt), sl(b3, kt), AF.Identity,
                                   bias=ccol("ln1b", kt), scale=ccol("ln1w", kt))
            P.step("scalar", f_xaf, reads=["b3", "cols"], writes=["b3"])

            emit_site_mms(spBC, "spBC", wo_wts, W2O, b5, "b5", "lnc", False)

            emit_split(b5, "b5", spA, "spA", with_sq=False)

            def f_pregate(eng, resA=resA):
                eng.tensor_tensor(b5[:], b5[:], b3[:], OP.subtract)
                eng.drain()
                eng.tensor_tensor(resA[:], resA[:], b3[:], OP.add)
                eng.drain()
            P.step("vector", f_pregate, reads=["b5", "b3", rn], writes=["b5", rn])

            nx1 = c < NCH - 1
            if nx1:
                def f_xl(eng, c=c):
                    eng.dma_start(out=b3.ap().rearrange("p (kt s) -> p kt s", s=CH),
                                  in_=xv[c + 1]).then_inc(q_x, 16)
                P.step("gpsimd", f_xl, writes=["b3"], dma_sem="q_x")
                emit_split(b3, "b3", spBC, "spBC")

            wgb_rhs = [("spA", sp_hi(spA, k), sp_lo(spA, k)) for k in range(8)]
            wgb_wts = [wg_w(8 + k) for k in range(8)]
            mk_main(wgb_rhs, wgb_wts, [0, 1, 2, 3], 0, ks=[0, 1, 2, 3])
            mk_main(wgb_rhs, wgb_wts, [0, 1, 2, 3], 0, close=True,
                    ks=[4, 5, 6, 7], opens=False)
            mk_main(wgb_rhs, wgb_wts, [4, 5, 6, 7], CH, ks=[0, 1, 2, 3])
            mk_main(wgb_rhs, wgb_wts, [4, 5, 6, 7], CH, close=True,
                    ks=[4, 5, 6, 7], opens=False)

            def f_wgbev(eng, zG=zG):
                for mt in range(8):
                    reg = 0 if mt < 4 else CH
                    eng.tensor_tensor(sl(zG, mt), sl(zG, mt),
                                      pmain[mt % 4][:, reg:reg + CH], OP.add)
                eng.drain()
            P.step("vector", f_wgbev,
                   reads=["pm0", "pm1", "pm2", "pm3", zn], writes=[zn])

            if nx1:
                x_rhs = [("spBC", sp_hi(spBC, k), sp_lo(spBC, k)) for k in range(8)]
                emit_stats("spBC", spBC)
                mk_main(x_rhs, wffn_wts, [0, 1, 2, 3], 0)

            b2h = b2.ap().bitcast(BF16)

            def f_zsh(eng, zG=zG):
                eng.activation(b2h[:, 0:2048], zG[:], AF.Copy)
            P.step("scalar", f_zsh, reads=[zn], writes=["b2"])

            def f_zsl(eng, zG=zG):
                eng.tensor_tensor(b2h[:, 2048:4096], zG[:], b2h[:, 0:2048], OP.subtract)
                eng.drain()
            P.step("vector", f_zsl, reads=[zn, "b2"], writes=["b2"])

            def f_zsq(eng, zG=zG):
                eng.activation(sq_sb[:], zG[:], AF.Square)
            P.step("scalar", f_zsq, reads=[zn], writes=["sq"])

            if nx1:
                emit_rows("mac", True)

            def f_zst(eng):
                ins = None
                for half in range(2):
                    for kt in range(8):
                        eng.matmul(pstat[:, 0:CH], ones_bf[:],
                                   b2h[:, 2048 * half + CH * kt:2048 * half + CH * (kt + 1)],
                                   start=(half == 0 and kt == 0), stop=(half == 1 and kt == 7))
                for kt in range(8):
                    ins = eng.matmul(pstat[:, CH:2 * CH], ones_r[:],
                                     sq_sb[:, CH * kt:CH * (kt + 1)],
                                     start=(kt == 0), stop=(kt == 7))
                return ins
            P.step("tensor", f_zst, reads=["b2", "sq", "ones"], writes=["pstat"])

            if nx1:
                emit_bc(False, a2_sb, "a2_sb")
                mk_corr(W2F, [0, 1, 2, 3], 0)
                mk_main(x_rhs, wffn_wts, [4, 5, 6, 7], CH)
                mk_corr(W2F, [4, 5, 6, 7], CH)

            emit_rows("att", False, need_corr=False)
            emit_bc(True, mu_lng=True)

            def f_znm(eng, zG=zG):
                eng.tensor_tensor(v8(zG), v8(zG), bcast8(pmb[:, 0:CH]), OP.add)
                eng.drain()
                eng.tensor_tensor(v8(zG), v8(zG), bcast8(prb[:, 0:CH]), OP.mult)
                eng.drain()
            P.step("vector", f_znm, reads=[zn, "prb", "pmb"], writes=[zn])

            def f_sg(eng, zG=zG):
                for kt in range(8):
                    eng.activation(sl(zG, kt), sl(zG, kt), AF.Sigmoid,
                                   bias=ccol("lngb", kt), scale=ccol("lngw", kt))
            P.step("scalar", f_sg, reads=[zn, "cols"], writes=[zn])

            def f_gate(eng, resA=resA, zG=zG):
                eng.tensor_tensor(b5[:], zG[:], b5[:], OP.mult)
                eng.drain()
                eng.tensor_tensor(resA[:], resA[:], b5[:], OP.add)
                eng.drain()
            P.step("vector", f_gate, reads=["b5", zn, rn],
                   writes=[rn, "b5"])

            if nx1:
                mk_evac(b2, "b2", [0, 1, 2, 3], 0, a2_sb, "a2_sb")
                mk_evac(b2, "b2", [4, 5, 6, 7], CH, a2_sb, "a2_sb")

                def f_ad1n(eng, zG=zG):
                    eng.tensor_tensor(zG[:], b3[:], b2[:], OP.add)
                    eng.drain()
                P.step("vector", f_ad1n, reads=["b3", "b2", zn], writes=[zn])

            emit_split(resA, rn, spBC, "spBC")
            f2_rhs = [("spBC", sp_hi(spBC, k), sp_lo(spBC, k)) for k in range(8)]
            emit_stats("spBC", spBC)
            emit_rows("mac", True)
            mk_main(f2_rhs, wffn_wts, [0, 1, 2, 3], 0)
            emit_bc(False)
            if nx1:
                emit_split(zG, zn, spA, "spA")
            mk_corr(W2F, [0, 1, 2, 3], 0)
            mk_main(f2_rhs, wffn_wts, [4, 5, 6, 7], CH)
            mk_corr(W2F, [4, 5, 6, 7], CH)
            mk_evac(b2, "b2", [0, 1, 2, 3], 0)
            mk_evac(b2, "b2", [4, 5, 6, 7], CH)

            def f_y3(eng, resA=resA):
                eng.tensor_tensor(resA[:], resA[:], b2[:], OP.add)
                eng.drain()
            P.step("vector", f_y3, reads=["b2", rn], writes=[rn])

            def f_ys(eng, c=c, resA=resA):
                eng.dma_start(out=yv[c],
                              in_=resA.ap().rearrange("p (kt s) -> p kt s", s=CH)
                              ).then_inc(q_y, 16)
            last_ystore = P.step("gpsimd", f_ys, reads=[rn], dma_sem="q_y")

        P.step("gpsimd", lambda eng: None, extra=[last_ystore])

        P.finalize()

        @block.sync
        def _(sync):
            P.emit("sync", sync, sems, qsems)

        @block.gpsimd
        def _(gp):
            P.emit("gpsimd", gp, sems, qsems)

        @block.tensor
        def _(tensor):
            P.emit("tensor", tensor, sems, qsems)

        @block.vector
        def _(vector):
            P.emit("vector", vector, sems, qsems)

        @block.scalar
        def _(scalar):
            P.emit("scalar", scalar, sems, qsems)

    return nc


def host_constants():
    Pm = _sigma()
    pis = _pi_consts()
    t32 = np.linspace(2.0, 8.0, 1024, dtype=np.float32)
    dec32 = (np.float32(0.5) ** (np.float32(1.0) / t32)).astype(np.float32)
    beta32 = (np.float32(1.0) - dec32).astype(np.float32)
    return Pm, pis, dec32, beta32


def kernel(L0_attn_x, L0_attn_Wt, L0_attn_Wo, L0_attn_Wg, L0_attn_bg,
           L0_attn_ln1_w, L0_attn_ln1_b, L0_attn_lnc_w, L0_attn_lnc_b,
           L0_attn_lng_w, L0_attn_lng_b, L0_ffn_G1_R, L0_ffn_G2_R,
           L0_ffn_G1_I, L0_ffn_G2_I, L0_ffn_bias_real, L0_ffn_bias_imag):
    Pm, pis, dec32, beta32 = host_constants()
    f32 = np.float32

    x = np.asarray(L0_attn_x, dtype=f32)
    WtP = np.ascontiguousarray(np.asarray(L0_attn_Wt, dtype=f32)[Pm, :])
    WoP = np.ascontiguousarray(np.asarray(L0_attn_Wo, dtype=f32)[:, Pm])
    Pg = np.concatenate([Pm, 1024 + Pm])
    WgP = np.ascontiguousarray(np.asarray(L0_attn_Wg, dtype=f32)[Pg][:, Pm])

    biasf = np.concatenate([np.asarray(L0_ffn_bias_real, dtype=f32),
                            np.asarray(L0_ffn_bias_imag, dtype=f32)])[Pm]
    bgP = np.asarray(L0_attn_bg, dtype=f32)[Pm]
    rowsc = np.ascontiguousarray(np.stack([biasf, beta32, bgP]).astype(f32))
    cols = np.zeros((128, _COL_W), dtype=f32)

    def put(name, vec):
        cols[:, _COL_OFF[name]:_COL_OFF[name] + 8] = _col_pack(vec)

    put("ln1w", np.asarray(L0_attn_ln1_w, dtype=f32)[Pm])
    put("ln1b", np.asarray(L0_attn_ln1_b, dtype=f32)[Pm])
    put("lngw", np.asarray(L0_attn_lng_w, dtype=f32)[Pm])
    put("lngb", np.asarray(L0_attn_lng_b, dtype=f32)[Pm])
    put("bg", np.asarray(L0_attn_bg, dtype=f32)[Pm])
    put("lncw", np.asarray(L0_attn_lnc_w, dtype=f32))
    put("lncb", np.asarray(L0_attn_lnc_b, dtype=f32))
    put("dec", dec32)

    G1R = np.asarray(L0_ffn_G1_R, dtype=f32)[0]
    G2R = np.asarray(L0_ffn_G2_R, dtype=f32)[:, :, 0, :]
    G1I = np.asarray(L0_ffn_G1_I, dtype=f32)[0]
    G2I = np.asarray(L0_ffn_G2_I, dtype=f32)[:, :, 0, :]
    g2l_r = np.ascontiguousarray(G2R.transpose(0, 2, 1).reshape(4, 256))
    g2l_i = np.ascontiguousarray(G2I.transpose(0, 2, 1).reshape(4, 256))
    g1r_r = np.ascontiguousarray(G1R.transpose(1, 0, 2).reshape(4, 1024))
    g1r_i = np.ascontiguousarray(G1I.transpose(1, 0, 2).reshape(4, 1024))

    nc = build_program()

    in_maps = []
    for i in range(NCORE):
        xT = np.ascontiguousarray(x[i][:, Pm].T)
        in_maps.append({
            "x": xT, "wt": WtP, "wo": WoP, "wg": WgP,
            "cols": cols, "rowsc": rowsc,
            "pis": pis.reshape(8 * 128, 128),
            "g2l_r": g2l_r, "g2l_i": g2l_i, "g1r_r": g1r_r, "g1r_i": g1r_i,
        })

    res = run_bass_kernel_spmd(nc, in_maps, core_ids=list(range(NCORE)))

    out = np.empty((NCORE, SEQ, D), dtype=f32)
    for i in range(NCORE):
        yT = np.asarray(res.results[i]["y"])
        out[i][:, Pm] = yT.T
    return out
